# version 25
# baseline (speedup 1.0000x reference)
import numpy as np
import ml_dtypes
from contextlib import ExitStack

import concourse.bass as bass
import concourse.mybir as mybir
from concourse.bass_utils import run_bass_kernel_spmd

F32 = mybir.dt.float32
BF16 = mybir.dt.bfloat16
AF = mybir.ActivationFunctionType
ALU = mybir.AluOpType

NCORES = 8

PE, DV, AC, SY, G = "pe", "dv", "ac", "sy", "g"
ENGINES = (PE, DV, AC, SY, G)


class Prog:

    def __init__(self):
        self.ops = []
        self.streams = {}
        self.last_write = {}
        self.readers = {}
        self.eng_count = {e: 0 for e in ENGINES}

    def stream(self, name, bulk=False, cc=False):
        if name not in self.streams:
            self.streams[name] = {"bulk": bulk, "cc": cc, "count": 0}

    def op(self, eng, fn, reads=(), writes=(), dma=None, cc=False, tag=""):
        o = {"eng": eng, "fn": fn, "reads": tuple(reads), "writes": tuple(writes),
             "dma": dma, "cc": cc, "tag": tag}
        if dma is not None:
            self.stream(dma, cc=cc)
            self.streams[dma]["count"] += 1
            o["scnt"] = self.streams[dma]["count"]
            o["dim"] = ("s", dma)
        else:
            self.eng_count[eng] += 1
            o["ecnt"] = self.eng_count[eng]
            o["dim"] = ("e", eng)
        self.ops.append(o)
        tok = (o["dim"], o.get("scnt") or o.get("ecnt"))
        for r in o["reads"]:
            self.readers.setdefault(r, []).append(tok)
        for w in o["writes"]:
            self.last_write[w] = tok
            self.readers[w] = []
        o["tok"] = tok
        return o

    def resolve(self):
        dims = [("e", e) for e in ENGINES] + [("s", s) for s in self.streams]
        dix = {d: i for i, d in enumerate(dims)}
        nd = len(dims)
        vc_of = {}
        cov = {e: [0] * nd for e in ENGINES}
        last_write = {}
        readers = {}
        self.need_inc = set()

        for o in self.ops:
            E = o["eng"]
            deps = {}

            def add_dep(tok):
                (dim, cnt) = tok
                i = dix[dim]
                if deps.get(i, 0) < cnt:
                    deps[i] = cnt

            for r in o["reads"]:
                t = last_write.get(r)
                if t:
                    add_dep(t)
            for w in o["writes"]:
                t = last_write.get(w)
                if t:
                    add_dep(t)
                for t2 in readers.get(w, ()):
                    add_dep(t2)

            waits = []
            mycov = cov[E]
            for i, cnt in sorted(deps.items()):
                dim = dims[i]
                if dim == ("e", PE) and E == PE:
                    continue
                if mycov[i] >= cnt:
                    continue
                waits.append((dim, cnt))
                if dim[0] == "e":
                    self.need_inc.add((dim[1], cnt))
                elif self.streams[dim[1]]["bulk"]:
                    mycov[i] = self.streams[dim[1]]["count"]
                pvc = vc_of[(dim, cnt)]
                for k in range(nd):
                    if pvc[k] > mycov[k]:
                        mycov[k] = pvc[k]
            o["waits"] = waits

            if o["dma"] is not None and not self.streams[o["dma"]]["bulk"] \
                    and not self.streams[o["dma"]].get("cc"):
                si = dix[("s", o["dma"])]
                prev = o["scnt"] - 1
                if prev > 0 and mycov[si] < prev:
                    o["waits"].append((("s", o["dma"]), prev))
                    mycov[si] = prev

            myvc = list(mycov)
            di = dix[o["dim"]]
            myvc[di] = o.get("scnt") or o.get("ecnt")
            vc_of[o["tok"]] = tuple(myvc)

            for r in o["reads"]:
                readers.setdefault(r, []).append(o["tok"])
            for w in o["writes"]:
                last_write[w] = o["tok"]
                readers[w] = []

        self.inc_val = {}
        counts = {e: 0 for e in ENGINES}
        per_eng_idx = {e: 0 for e in ENGINES}
        for o in self.ops:
            if o["dma"] is None:
                e = o["eng"]
                per_eng_idx[e] += 1
                if (e, per_eng_idx[e]) in self.need_inc:
                    counts[e] += 1
                    self.inc_val[(e, per_eng_idx[e])] = counts[e]
        self.eng_totals = counts

    def wait_value(self, dim, cnt):
        if dim[0] == "e":
            return self.inc_val[(dim[1], cnt)]
        s = self.streams[dim[1]]
        if s.get("cc"):
            return cnt
        if s["bulk"]:
            return 16 * s["count"]
        return 16 * cnt

    def emit(self, nc, sems):
        per_eng_ops = {e: [] for e in ENGINES}
        for o in self.ops:
            per_eng_ops[o["eng"]].append(o)

        def body(eng_obj, eng_id):
            ecnt = 0
            for o in per_eng_ops[eng_id]:
                for (dim, cnt) in o["waits"]:
                    sem = sems[dim[1]]
                    eng_obj.wait_ge(sem, self.wait_value(dim, cnt))
                try:
                    ins = o["fn"](eng_obj)
                except Exception:
                    import sys
                    print(f"EMIT FAIL at op tag={o['tag']} eng={eng_id}", file=sys.stderr)
                    raise
                if o["dma"] is not None:
                    ins.then_inc(sems[o["dma"]], 1 if o["cc"] else 16)
                else:
                    ecnt += 1
                    v = self.inc_val.get((eng_id, ecnt))
                    if v is not None:
                        ins.then_inc(sems[eng_id], 1)
        return body


def build_program(cfg):
    KC = cfg["KC"]
    HID = KC * 128
    NH = cfg["NH"]
    T = cfg["T"]
    NT = cfg["NT"]
    NC = cfg.get("NC", 8)
    BPC = KC // NC
    QPC = max(NH // NC, 1)
    TOK = T * NT
    E, R = 8, 16
    QD = NH * 128
    WCH = max(KC, 4 * NH)

    nc = bass.Bass("TRN2", target_bir_lowering=False)

    d_xT = nc.dram_tensor("xT", [HID, TOK], F32, kind="ExternalInput")
    d_wr = nc.dram_tensor("WrT", [HID, E], F32, kind="ExternalInput")
    d_la = nc.dram_tensor("lA", [HID, 128], F32, kind="ExternalInput")
    d_lb = nc.dram_tensor("lB", [128, HID], F32, kind="ExternalInput")
    d_m = nc.dram_tensor("Mt", [NH, 128, 128], F32, kind="ExternalInput")
    d_zd = nc.dram_tensor("Zd", [NH, 128, E], F32, kind="ExternalInput")
    d_bq = nc.dram_tensor("bqv", [128, NH], F32, kind="ExternalInput")
    d_bb = nc.dram_tensor("bbv", [128, KC], F32, kind="ExternalInput")
    d_br = nc.dram_tensor("brv", [E, 1], F32, kind="ExternalInput")
    d_c1 = nc.dram_tensor("c_ones128", [128, 1], F32, kind="ExternalInput")
    d_cr1 = nc.dram_tensor("c_r1", [8, 64], F32, kind="ExternalInput")
    d_cr2 = nc.dram_tensor("c_r2", [8, 64], F32, kind="ExternalInput")
    d_cg = nc.dram_tensor("c_g1", [64, E], F32, kind="ExternalInput")
    d_ce = nc.dram_tensor("c_exp8", [E, 128], F32, kind="ExternalInput")
    d_cs = nc.dram_tensor("c_sel8", [E, NH * 128], F32, kind="ExternalInput")
    d_co = nc.dram_tensor("c_ones_r", [1, 128], F32, kind="ExternalInput")
    d_y = nc.dram_tensor("yT", [HID, TOK], F32, kind="ExternalOutput")
    d_wof = nc.dram_tensor("WOF", [KC, 128, NH * 128], BF16, kind="Internal",
                           addr_space="Shared")
    d_wb2 = nc.dram_tensor("WB2", [KC, 128, KC * 128], BF16, kind="Internal",
                           addr_space="Shared")
    d_wq2 = nc.dram_tensor("WQ2", [max(NH, NC), 128, KC * 128], BF16, kind="Internal",
                           addr_space="Shared")
    d_wbs = nc.dram_tensor("WbS", [HID, BPC * 128], F32, kind="ExternalInput")
    d_wqs = nc.dram_tensor("WqS", [HID, QPC * 128], F32, kind="ExternalInput")
    d_wos = nc.dram_tensor("WoS", [4 * QD, BPC * 128], F32, kind="ExternalInput")
    d_wb2s = nc.dram_tensor("WB2S", [BPC, 128, KC * 128], BF16, kind="Internal")
    d_wq2s = nc.dram_tensor("WQ2S", [QPC, 128, KC * 128], BF16, kind="Internal")
    d_wofs = nc.dram_tensor("WOFS", [BPC, 128, NH * 128], BF16, kind="Internal")

    p = Prog()
    for s in ("cb", "gb"):
        p.stream(s, bulk=True)
    for s in ([f"ws{k}g" for k in range(3)]
              + [f"ws{k}a" for k in range(3)] + [f"ws{k}b" for k in range(3)]
              + ["wfA", "wfB", "wfw"] + [f"wf{k}" for k in range(3)]
              + [f"sx{k}" for k in range(8)] + [f"yst{k}" for k in range(8)]):
        p.stream(s)

    ctx = ExitStack()
    sb = lambda name, shape, dt: ctx.enter_context(nc.sbuf_tensor(name, shape, dt))
    pst = lambda name: ctx.enter_context(nc.psum_tensor(name, [128, T], F32))

    xs = sb("xs", [128, KC * T], F32)
    xn = sb("xn", [128, KC * T], BF16)
    xq = sb("xq", [128, 2 * T], BF16)
    accs = sb("accs", [128, T], F32)
    accq = sb("accq", [128, T], F32)
    ash = sb("ash", [128, T], BF16)
    asl = sb("asl", [128, T], BF16)
    aqh = sb("aqh", [128, T], BF16)
    aql = sb("aql", [128, T], BF16)
    tmp = sb("tmp", [128, T], BF16)
    nmx = sb("nmx", [128, T], BF16)
    tmpf32 = sb("tmpf32", [128, T], F32)
    wsb = [sb(f"ws{k}", [128, WCH * 128], BF16) for k in range(3)]
    WFW = max(NH, 8)
    wfsb = [sb(f"wfs{k}", [128, WFW * 128], BF16) for k in range(3)]
    wofsb = sb("wofsb", [128, NH * 128], BF16)
    t1r = sb("t1r", [128, 2 * T], F32)
    qt = sb("qt", [128, NH * T], BF16)
    wexp = sb("wexp", [128, T], BF16)
    hw = sb("hw", [128, T], BF16)
    la = sb("la", [128, KC * 128], BF16)
    lb = sb("lb", [128, KC * 128], BF16)
    wrs = sb("wrs", [128, KC * E], BF16)
    m8 = sb("m8", [128, NH * 128], BF16)
    zd8 = sb("zd8", [128, NH * E], BF16)
    bqs = sb("bqs", [128, NH], F32)
    bbs = sb("bbs", [128, KC], F32)
    hbb = sb("hbb", [128, KC], F32)
    brs = sb("brs", [E, 1], F32)
    epsb = sb("epsb", [128, 2], F32)
    ones128 = sb("ones128", [128, 1], BF16)
    onesr = sb("onesr", [1, 128], BF16)
    r1s = sb("r1s", [8, 64], BF16)
    r2s = sb("r2s", [8, 64], BF16)
    g1 = sb("g1", [64, E], BF16)
    exp8 = sb("exp8", [E, 128], BF16)
    sel8 = sb("sel8", [E, NH * 128], BF16)
    ones8 = sb("ones8", [E, 1], BF16)
    mu_t = sb("mu_t", [1, T], F32)
    msq_t = sb("msq_t", [1, T], F32)
    rstd_t = sb("rstd_t", [1, T], F32)
    nm_t = sb("nm_t", [1, T], F32)
    tmpf_t = sb("tmpf_t", [1, T], F32)
    t1s_t = rstd_t
    kzr = nm_t
    shi = sb("shi", [1, T], BF16)
    slo = sb("slo", [1, T], BF16)
    nhi = sb("nhi", [1, T], BF16)
    nlo = sb("nlo", [1, T], BF16)
    thi = sb("thi", [1, T], BF16)
    tlo = sb("tlo", [1, T], BF16)
    zb = sb("zb", [E, T], BF16)
    ge = sb("ge", [64, T], BF16)
    az = sb("az", [64, T], BF16)
    rz = sb("rz", [E, T], F32)
    supp = sb("supp", [E, T], BF16)
    zsup = sb("zsup", [E, T], BF16)
    d1 = sb("d1", [E, T], F32)
    wsp = sb("wsp", [E, T], BF16)
    recs = sb("recs", [E, T], F32)

    PS = [pst(f"PS{b}") for b in range(8)]

    xs3 = xs[:].rearrange("p (k t) -> p k t", t=T)
    xn3 = xn[:].rearrange("p (k t) -> p k t", t=T)
    xq3 = xq[:].rearrange("p (k t) -> p k t", t=T)
    ws3 = [w[:].rearrange("p (k c) -> p k c", c=128) for w in wsb]
    wfs3 = [w[:, 0:NH * 128].rearrange("p (e c) -> p e c", c=128) for w in wfsb]
    t1r3 = t1r[:].rearrange("p (i t) -> p i t", t=T)
    qt3 = qt[:].rearrange("p (e t) -> p e t", t=T)
    la3 = la[:].rearrange("p (k c) -> p k c", c=128)
    lb3 = lb[:].rearrange("p (k c) -> p k c", c=128)
    wr3 = wrs[:].rearrange("p (k c) -> p k c", c=E)
    m83 = m8[:].rearrange("p (e c) -> p e c", c=128)
    zd83 = zd8[:].rearrange("p (e c) -> p e c", c=E)

    def ld(out_ap, in_ap, writes, tag):
        p.op(SY, lambda e, o=out_ap, i=in_ap: e.dma_start(out=o, in_=i),
             writes=writes, dma="cb", tag=tag)

    ld(bqs[:, :], d_bq[:, :], [("bqs",)], "bq")
    ld(bbs[:, :], d_bb[:, :], [("bbs",)], "bb")
    ld(brs[:, :], d_br[:, :], [("brs",)], "br")

    def gld(out_ap, in_ap, writes, tag):
        p.op(G, lambda e, o=out_ap, i=in_ap: e.dma_start(out=o, in_=i),
             writes=writes, dma="gb", tag=tag)

    gld(ones128[:, :], d_c1[:, :], [("ones128",)], "c1")
    gld(onesr[:, :], d_co[:, :], [("onesr",)], "co")
    gld(r1s[:, :], d_cr1[:, :], [("r1s",)], "cr1")
    gld(r2s[:, :], d_cr2[:, :], [("r2s",)], "cr2")
    gld(g1[:, :], d_cg[:, :], [("g1",)], "cg")
    gld(exp8[:, :], d_ce[:, :], [("exp8",)], "ce")
    gld(sel8[:, :], d_cs[:, :], [("sel8",)], "cs")
    gld(la3, d_la[:].rearrange("(k p) c -> p k c", p=128), [("la",)], "la")
    gld(lb[:, :], d_lb[:, :], [("lb",)], "lb")
    gld(wr3, d_wr[:].rearrange("(k p) c -> p k c", p=128), [("wrs",)], "wr")
    gld(m83, d_m[:].rearrange("e p c -> p e c"), [("m8",)], "m")
    gld(zd83, d_zd[:].rearrange("e p c -> p e c"), [("zd8",)], "zd")

    p.op(DV, lambda e: e.memset(epsb[:, 0:1], 1e-5), writes=[("epsL",)])
    p.op(DV, lambda e: e.memset(epsb[:, 1:2], 1e-6), writes=[("epsD",)])
    p.op(DV, lambda e: e.memset(ones8[:, :], 1.0), writes=[("ones8",)])
    p.op(DV, lambda e: e.tensor_scalar_mul(hbb[:, :], bbs[:, :], 0.5),
         reads=[("bbs",)], writes=[("hbb",)])

    wpar = [0]

    def _next_par():
        par = wpar[0] % 3
        wpar[0] += 1
        return par

    wfpar = [0]
    wopar = [0]

    def stream_cast(src_ap, tag):
        par = _next_par()
        nch = src_ap.shape[1]
        p.op(G, lambda e, o=ws3[par][:, 0:nch], i=src_ap: e.dma_start(out=o, in_=i),
             writes=[("ws", par, 0), ("ws", par, 1)], dma=f"ws{par}g", tag=tag)
        return par

    def spill(par, nch, scratch_ap, res, sem, tag):
        p.op(G, lambda e, o=scratch_ap, i=ws3[par].rearrange("p k c -> p (k c)")[:, 0:nch * 128]:
             e.dma_start(out=o, in_=i),
             reads=[("ws", par, 0), ("ws", par, 1)], writes=[res], dma=sem, tag=tag)

    def stream_scratch(scratch_ap, res_a, res_b, nch, tag, second_q=G):
        par = _next_par()
        hh = (nch // 2) * 128
        flat = ws3[par].rearrange("p k c -> p (k c)")
        sflat = scratch_ap.rearrange("p c -> p c") if len(scratch_ap.shape) == 2 else scratch_ap
        p.op(SY, lambda e, o=flat[:, 0:hh], i=sflat[:, 0:hh]: e.dma_start(out=o, in_=i),
             reads=[res_a], writes=[("ws", par, 0)], dma=f"ws{par}a", tag=tag + "a")
        p.op(second_q, lambda e, o=flat[:, hh:2 * hh], i=sflat[:, hh:2 * hh]: e.dma_start(out=o, in_=i),
             reads=[res_b], writes=[("ws", par, 1)], dma=f"ws{par}b", tag=tag + "b")
        return par

    def stream_wb(i, call, j):
        q2 = SY if (i == 0 and call == 1) else G
        return stream_scratch(d_wb2[j], ("wb2", j, 0), ("wb2", j, 1), KC,
                              f"t{i}f{call}wb{j}", second_q=q2)

    def stream_wq(i, e_):
        return stream_scratch(d_wq2[e_], ("wq2", e_), ("wq2", e_), KC,
                              f"t{i}wq{e_}")

    wbs_blocks = d_wbs[:].rearrange("(k p) (j c) -> j p k c", p=128, c=128)
    wqs_blocks = d_wqs[:].rearrange("(k p) (j c) -> j p k c", p=128, c=128)
    wos_blocks = d_wos[:].rearrange("(k p) (j c) -> j p k c", p=128, c=128)

    def fold_block_local(jj):
        for q in range(NH // 2):
            par = wfpar[0] % 2
            wfpar[0] += 1
            sem = "wfA" if par == 0 else "wfB"
            buf = wfsb[par]
            p.op(G, lambda e, o=buf[:, 0:8 * 128], i=wos_blocks[jj][:, 8 * q:8 * q + 8]:
                 e.dma_start(out=o, in_=i),
                 writes=[("wf", par)], dma=sem, tag=f"foldc{jj}.{q}")
            stg = buf[:].rearrange("p (el g c) -> p el g c", g=4, c=128)
            out_v = wofsb[:, 2 * q * 128:(2 * q + 2) * 128].rearrange(
                "p (el c) -> p el c", c=128)
            for g in range(3):
                in0 = out_v if g > 0 else stg[:, :, 0]
                in1 = stg[:, :, g + 1]
                p.op(DV, lambda e, a=in0, b=in1, o=out_v:
                     e.tensor_tensor(o, a, b, ALU.add),
                     reads=[("wf", par)] + ([("wofsb", q)] if g > 0 else []),
                     writes=[("wofsb", q)], tag=f"folda{jj}.{q}.{g}")
        p.op(G, lambda e, jj=jj: e.dma_start(out=d_wofs[jj], in_=wofsb[:, :]),
             reads=[("wofsb", q) for q in range(NH // 2)],
             writes=[("wofs_s", jj)], dma="wfw", tag=f"foldst{jj}")
        _emit_cc(f"cco{jj}", d_wofs[jj:jj + 1], d_wof[jj * NC:(jj + 1) * NC],
                 [("wofs_s", jj)],
                 [("wof_s", jx) for jx in range(jj * NC, (jj + 1) * NC)])

    def precast_phase():
        def wb_group(jj):
            par = stream_cast(wbs_blocks[jj], f"wbc{jj}")
            spill(par, KC, d_wb2s[jj], ("wb2s", jj), f"wbw{jj}", f"wbspill{jj}")
            _emit_cc(f"ccb{jj}", d_wb2s[jj:jj + 1], d_wb2[jj * NC:(jj + 1) * NC],
                     [("wb2s", jj)],
                     [("wb2", j, h) for j in range(jj * NC, (jj + 1) * NC)
                      for h in range(2)])
        wb_group(0)
        for jj in range(QPC):
            par = stream_cast(wqs_blocks[jj], f"wqc{jj}")
            spill(par, KC, d_wq2s[jj], ("wq2s", jj), f"wqw{jj}", f"wqspill{jj}")
        _emit_cc("ccq", d_wq2s, d_wq2, [("wq2s", jj) for jj in range(QPC)],
                 [("wq2", e_) for e_ in range(NH)])
        for jj in range(1, BPC):
            wb_group(jj)

    def _emit_cc(name, src_t, dst_t, rds, wrs):
        p.op(G, lambda e, s=src_t, d=dst_t: e.collective_compute(
            "AllGather", ALU.bypass, replica_groups=[list(range(NC))],
            ins=[s[:].rearrange("j p c -> (j p) c")],
            outs=[d[:].rearrange("j p c -> (j p) c")]),
            reads=rds, writes=wrs, dma=name, cc=True, tag=name)


    def wof_block(i, j):
        par = wopar[0] % 3
        q = SY if wopar[0] % 2 == 0 else G
        wopar[0] += 1
        p.op(q, lambda e, j=j, d=wfsb[par]: e.dma_start(
            out=d[:, 0:NH * 128], in_=d_wof[j]),
            reads=[("wof_s", j)], writes=[("wf", par)], dma=f"wf{par}", tag=f"wof{i}.{j}")
        return par

    def ln_stats(tg, k):
        rk = k % 2
        p.op(AC, lambda e, k=k, rk=rk: e.activation(xq3[:, rk], xs3[:, k], AF.Square),
             reads=[("xs", k)], writes=[("xq", rk)], tag=f"{tg}q{k}")
        if k == 0:
            p.op(DV, lambda e, k=k: e.tensor_copy(accs[:, :], xs3[:, k]),
                 reads=[("xs", k)], writes=[("accs",)], tag=f"{tg}s{k}")
            p.op(DV, lambda e, rk=rk: e.tensor_copy(accq[:, :], xq3[:, rk]),
                 reads=[("xq", rk)], writes=[("accq",)], tag=f"{tg}sq{k}")
        else:
            p.op(DV, lambda e, k=k: e.tensor_tensor(
                accs[:, :], accs[:, :], xs3[:, k], ALU.add),
                reads=[("xs", k), ("accs",)], writes=[("accs",)], tag=f"{tg}s{k}")
            p.op(DV, lambda e, rk=rk: e.tensor_tensor(
                accq[:, :], accq[:, :], xq3[:, rk], ALU.add),
                reads=[("xq", rk), ("accq",)], writes=[("accq",)], tag=f"{tg}sq{k}")

    def ln_finish(tg):
        p.op(DV, lambda e: e.tensor_copy(ash[:, :], accs[:, :]),
             reads=[("accs",)], writes=[("ash",)], tag=f"{tg}ah")
        p.op(DV, lambda e: e.tensor_tensor(asl[:, :], accs[:, :], ash[:, :],
                                           ALU.subtract),
             reads=[("accs",), ("ash",)], writes=[("asl",)], tag=f"{tg}al")
        p.op(DV, lambda e: e.tensor_copy(aqh[:, :], accq[:, :]),
             reads=[("accq",)], writes=[("aqh",)], tag=f"{tg}qh")
        p.op(DV, lambda e: e.tensor_tensor(aql[:, :], accq[:, :], aqh[:, :],
                                           ALU.subtract),
             reads=[("accq",), ("aqh",)], writes=[("aql",)], tag=f"{tg}ql")
        p.op(PE, lambda e: e.matmul(PS[2][0:1, :], ones128[:, :], ash[:, :],
                                    start=True, stop=False),
             reads=[("ash",), ("ones128",)], writes=[("ps", 2)], tag=f"{tg}s1")
        p.op(PE, lambda e: e.matmul(PS[2][0:1, :], ones128[:, :], asl[:, :],
                                    start=False, stop=True),
             reads=[("asl",)], writes=[("ps", 2)], tag=f"{tg}s2")
        p.op(PE, lambda e: e.matmul(PS[3][0:1, :], ones128[:, :], aqh[:, :],
                                    start=True, stop=False),
             reads=[("aqh",)], writes=[("ps", 3)], tag=f"{tg}s3")
        p.op(PE, lambda e: e.matmul(PS[3][0:1, :], ones128[:, :], aql[:, :],
                                    start=False, stop=True),
             reads=[("aql",)], writes=[("ps", 3)], tag=f"{tg}s4")
        p.op(DV, lambda e: e.tensor_scalar_mul(mu_t[:, :], PS[2][0:1, :], 1.0),
             reads=[("ps", 2)], writes=[("mu",)], tag=f"{tg}mu")
        p.op(DV, lambda e: e.tensor_mul(tmpf_t[:, :], mu_t[:, :], mu_t[:, :]),
             reads=[("mu",)], writes=[("tmpf",)], tag=f"{tg}m2")
        p.op(DV, lambda e: e.tensor_sub(tmpf_t[:, :], PS[3][0:1, :], tmpf_t[:, :]),
             reads=[("ps", 3), ("tmpf",)], writes=[("tmpf",)], tag=f"{tg}var")
        p.op(DV, lambda e: e.tensor_scalar_add(tmpf_t[:, :], tmpf_t[:, :], 1e-5),
             reads=[("tmpf",)], writes=[("tmpf",)], tag=f"{tg}ve")
        p.op(DV, lambda e: e.reciprocal(msq_t[:, :], tmpf_t[:, :]),
             reads=[("tmpf",)], writes=[("msq",)], tag=f"{tg}rc")
        p.op(AC, lambda e: e.activation(rstd_t[:, :], msq_t[:, :], AF.Sqrt),
             reads=[("msq",)], writes=[("rstd",)], tag=f"{tg}rs")
        p.op(DV, lambda e: e.tensor_copy(shi[:, :], rstd_t[:, :]),
             reads=[("rstd",)], writes=[("rstd_hi",)], tag=f"{tg}h1")
        p.op(DV, lambda e: e.tensor_sub(slo[:, :], rstd_t[:, :], shi[:, :]),
             reads=[("rstd",), ("rstd_hi",)], writes=[("rstd_lo",)], tag=f"{tg}h3")
        p.op(PE, lambda e: e.matmul(PS[4][:, :], onesr[:, :], shi[:, :],
                                    start=True, stop=False),
             reads=[("rstd_hi",), ("onesr",)], writes=[("ps", 4)], tag=f"{tg}e1")
        p.op(PE, lambda e: e.matmul(PS[4][:, :], onesr[:, :], slo[:, :],
                                    start=False, stop=True),
             reads=[("rstd_lo",)], writes=[("ps", 4)], tag=f"{tg}e2")
        p.op(DV, lambda e: e.scalar_tensor_tensor(nm_t[:, :], mu_t[:, :], -1.0,
                                                  rstd_t[:, :], ALU.mult, ALU.mult),
             reads=[("mu",), ("rstd",)], writes=[("nm",)], tag=f"{tg}nm")
        p.op(DV, lambda e: e.tensor_copy(nhi[:, :], nm_t[:, :]),
             reads=[("nm",)], writes=[("nm_hi",)], tag=f"{tg}h4")
        p.op(DV, lambda e: e.tensor_sub(nlo[:, :], nm_t[:, :], nhi[:, :]),
             reads=[("nm",), ("nm_hi",)], writes=[("nm_lo",)], tag=f"{tg}h6")
        p.op(PE, lambda e: e.matmul(PS[5][:, :], onesr[:, :], nhi[:, :],
                                    start=True, stop=False),
             reads=[("nm_hi",)], writes=[("ps", 5)], tag=f"{tg}e3")
        p.op(PE, lambda e: e.matmul(PS[5][:, :], onesr[:, :], nlo[:, :],
                                    start=False, stop=True),
             reads=[("nm_lo",)], writes=[("ps", 5)], tag=f"{tg}e4")
        p.op(AC, lambda e: e.activation(nmx[:, :], PS[5][:, :], AF.Copy),
             reads=[("ps", 5)], writes=[("nmx",)], tag=f"{tg}nmx")
        for k in range(KC):
            p.op(DV, lambda e, k=k: e.tensor_tensor(tmp[:, :], xs3[:, k], PS[4][:, :],
                                                    ALU.mult),
                 reads=[("xs", k), ("ps", 4)], writes=[("tmp",)], tag=f"{tg}x1.{k}")
            p.op(DV, lambda e, k=k: e.tensor_tensor(xn3[:, k], tmp[:, :], nmx[:, :],
                                                    ALU.add),
                 reads=[("tmp",), ("nmx",)], writes=[("xn", k)], tag=f"{tg}x2.{k}")

    def ffn_phase(i, call, hook=None):
        tg = f"t{i}f{call}"
        for k in range(KC):
            p.op(PE, lambda e, k=k: e.matmul(PS[6][0:E, :], wr3[:, k], xn3[:, k],
                                             start=(k == 0), stop=(k == KC - 1)),
                 reads=[("xn", k), ("wrs",)], writes=[("ps", 6)], tag=f"{tg}r{k}")
            p.op(PE, lambda e, k=k: e.matmul(PS[7][:, :], la3[:, k], xn3[:, k],
                                             start=(k == 0), stop=(k == KC - 1)),
                 reads=[("xn", k), ("la",)], writes=[("ps", 7)], tag=f"{tg}h{k}")
        p.op(AC, lambda e: e.activation(zb[:, :], PS[6][0:E, :], AF.Identity,
                                        bias=brs[:, :], scale=1.0),
             reads=[("ps", 6), ("brs",)], writes=[("zb",)], tag=f"{tg}z")
        p.op(PE, lambda e: e.matmul(PS[2][0:64, :], r1s[:, :], zb[:, :],
                                    start=True, stop=True),
             reads=[("zb",), ("r1s",)], writes=[("ps", 2)], tag=f"{tg}A")
        p.op(PE, lambda e: e.matmul(PS[3][0:64, :], r2s[:, :], zb[:, :],
                                    start=True, stop=True),
             reads=[("zb",), ("r2s",)], writes=[("ps", 3)], tag=f"{tg}B")
        p.op(AC, lambda e: e.activation(az[:, :], PS[3][0:64, :], AF.Copy),
             reads=[("ps", 3)], writes=[("az",)], tag=f"{tg}Bs")
        p.op(DV, lambda e: e.tensor_tensor(ge[:, :], PS[2][0:64, :], az[:, :], ALU.is_ge),
             reads=[("ps", 2), ("az",)], writes=[("ge",)], tag=f"{tg}ge")
        p.op(DV, lambda e: e.tensor_tensor(az[:, :], PS[2][0:64, :], ge[:, :], ALU.mult),
             reads=[("ps", 2), ("ge",)], writes=[("az",)], tag=f"{tg}az")

        def base_chain(j, b=None):
            par = stream_wb(i, call, j)
            b = j % 2 if b is None else b
            for k in range(KC):
                p.op(PE, lambda e, k=k, par=par, b=b: e.matmul(
                    PS[b][:, :], ws3[par][:, k], xn3[:, k],
                    start=(k == 0), stop=False),
                    reads=[("xn", k), ("ws", par, k // (KC // 2))],
                    writes=[("ps", b)], tag=f"{tg}m{j}.{k}")

        def close_chain(j, b=None):
            b = j % 2 if b is None else b
            p.op(PE, lambda e, j=j, b=b: e.matmul(
                PS[b][:, :], lb3[:, j], hw[:, :], start=False, stop=True),
                reads=[("hw",), ("lb",)], writes=[("ps", b)], tag=f"{tg}lo{j}")
            if hook is not None:
                hook(j, "pre")
            s = j % 2
            p.op(AC, lambda e, j=j, b=b, s=s: e.activation(
                t1r3[:, s], PS[b][:, :], AF.Identity, bias=hbb[:, j:j + 1], scale=0.5),
                reads=[("ps", b), ("hbb",)], writes=[("t1", s)], tag=f"{tg}ev{j}")
            p.op(DV, lambda e, j=j, s=s: e.tensor_tensor(
                xs3[:, j], xs3[:, j], t1r3[:, s], ALU.add),
                reads=[("xs", j), ("t1", s)], writes=[("xs", j)],
                tag=f"{tg}x{j}")
            if hook is not None:
                hook(j, "post")

        base_chain(0)
        p.op(PE, lambda e: e.matmul(PS[4][0:E, :], g1[:, :], ge[:, :],
                                    start=True, stop=True),
             reads=[("ge",), ("g1",)], writes=[("ps", 4)], tag=f"{tg}rr")
        p.op(PE, lambda e: e.matmul(PS[4][32:32 + E, :], g1[:, :], az[:, :],
                                    start=True, stop=True),
             reads=[("az",), ("g1",)], writes=[("ps", 4)], tag=f"{tg}cc")
        p.op(DV, lambda e: e.tensor_tensor(rz[:, :], PS[4][0:E, :], zb[:, :], ALU.mult),
             reads=[("ps", 4), ("zb",)], writes=[("rz",)], tag=f"{tg}rz")
        p.op(DV, lambda e: e.scalar_tensor_tensor(supp[:, :], rz[:, :], 1.0,
                                                  PS[4][32:32 + E, :], ALU.add, ALU.is_gt),
             reads=[("rz",), ("ps", 4)], writes=[("supp",)], tag=f"{tg}sp")
        p.op(DV, lambda e: e.tensor_tensor(zsup[:, :], zb[:, :], supp[:, :], ALU.mult),
             reads=[("zb",), ("supp",)], writes=[("zsup",)], tag=f"{tg}zs")
        base_chain(1)
        p.op(PE, lambda e: e.matmul(PS[5][0:1, :], ones8[:, :], supp[:, :],
                                    start=True, stop=True),
             reads=[("supp",), ("ones8",)], writes=[("ps", 5)], tag=f"{tg}kz")
        p.op(PE, lambda e: e.matmul(PS[5][32:33, :], ones8[:, :], zsup[:, :],
                                    start=True, stop=True),
             reads=[("zsup",), ("ones8",)], writes=[("ps", 5)], tag=f"{tg}S")
        p.op(DV, lambda e: e.tensor_scalar_add(t1s_t[:, :], PS[5][32:33, :], -1.0),
             reads=[("ps", 5)], writes=[("rstd",)], tag=f"{tg}S1")
        p.op(DV, lambda e: e.reciprocal(kzr[:, :], PS[5][0:1, :]),
             reads=[("ps", 5)], writes=[("nm",)], tag=f"{tg}kr")
        p.op(DV, lambda e: e.tensor_mul(t1s_t[:, :], t1s_t[:, :], kzr[:, :]),
             reads=[("rstd",), ("nm",)], writes=[("rstd",)], tag=f"{tg}tau")
        p.op(DV, lambda e: e.tensor_copy(thi[:, :], t1s_t[:, :]),
             reads=[("rstd",)], writes=[("tau_hi",)], tag=f"{tg}th")
        p.op(DV, lambda e: e.tensor_sub(tlo[:, :], t1s_t[:, :], thi[:, :]),
             reads=[("rstd",), ("tau_hi",)], writes=[("tau_lo",)], tag=f"{tg}tl")
        base_chain(2, b=3)
        p.op(PE, lambda e: e.matmul(PS[4][64:64 + E, :], onesr[0:1, 0:E], thi[:, :],
                                    start=True, stop=False),
             reads=[("tau_hi",), ("onesr",)], writes=[("ps", 4)], tag=f"{tg}te1")
        p.op(PE, lambda e: e.matmul(PS[4][64:64 + E, :], onesr[0:1, 0:E], tlo[:, :],
                                    start=False, stop=True),
             reads=[("tau_lo",)], writes=[("ps", 4)], tag=f"{tg}te2")
        p.op(DV, lambda e: e.tensor_tensor(d1[:, :], zb[:, :], PS[4][64:64 + E, :],
                                           ALU.subtract),
             reads=[("zb",), ("ps", 4)], writes=[("d1",)], tag=f"{tg}d1")
        p.op(DV, lambda e: e.tensor_scalar_max(wsp[:, :], d1[:, :], 0.0),
             reads=[("d1",)], writes=[("wsp",)], tag=f"{tg}w")
        p.op(PE, lambda e: e.matmul(PS[2][:, :], exp8[:, :], wsp[:, :],
                                    start=True, stop=True),
             reads=[("wsp",), ("exp8",), ("az",), ("ge",)], writes=[("ps", 2)],
             tag=f"{tg}we")
        p.op(AC, lambda e: e.activation(wexp[:, :], PS[2][:, :], AF.Copy),
             reads=[("ps", 2)], writes=[("wexp",)], tag=f"{tg}wc")
        p.op(DV, lambda e: e.tensor_tensor(hw[:, :], PS[7][:, :], wexp[:, :], ALU.mult),
             reads=[("ps", 7), ("wexp",)], writes=[("hw",)], tag=f"{tg}hw")
        close_chain(0)
        close_chain(1)
        close_chain(2, b=3)
        for j in range(3, KC):
            base_chain(j)
            close_chain(j)

    def attn_phase(i, hook=None):
        tg = f"t{i}a"
        for e_ in range(NH):
            par = stream_wq(i, e_)
            b = e_ % 2
            for k in range(KC):
                p.op(PE, lambda e, k=k, par=par, b=b: e.matmul(
                    PS[b][:, :], ws3[par][:, k], xn3[:, k],
                    start=(k == 0), stop=(k == KC - 1)),
                    reads=[("xn", k), ("ws", par, k // (KC // 2))],
                    writes=[("ps", b)], tag=f"{tg}q{e_}.{k}")
            p.op(DV, lambda e, e_=e_, b=b: e.tensor_scalar(
                tmpf32[:, :], PS[b][:, :], bqs[:, e_:e_ + 1], 0.0, ALU.add, ALU.min),
                reads=[("ps", b), ("bqs",)], writes=[("tmpf32",)], tag=f"{tg}em{e_}")
            p.op(AC, lambda e: e.activation(t1r3[:, 1], tmpf32[:, :], AF.Exp),
                 reads=[("tmpf32",)], writes=[("t1", 1)], tag=f"{tg}ee{e_}")
            p.op(DV, lambda e, e_=e_, b=b: e.tensor_scalar(
                t1r3[:, 0], PS[b][:, :], bqs[:, e_:e_ + 1], 0.0, ALU.add, ALU.max),
                reads=[("ps", b), ("bqs",)], writes=[("t1", 0)], tag=f"{tg}er{e_}")
            p.op(DV, lambda e, e_=e_: e.tensor_tensor(
                qt3[:, e_], t1r3[:, 1], t1r3[:, 0], ALU.add),
                reads=[("t1", 1), ("t1", 0)], writes=[("qt", e_)], tag=f"{tg}qt{e_}")
            p.op(PE, lambda e, e_=e_: e.matmul(
                PS[6][0:E, :], zd83[:, e_], qt3[:, e_],
                start=(e_ == 0), stop=(e_ == NH - 1)),
                reads=[("qt", e_), ("zd8",)], writes=[("ps", 6)], tag=f"{tg}dn{e_}")
            p.op(PE, lambda e, e_=e_: e.matmul(
                PS[7][:, :], m83[:, e_], qt3[:, e_], start=True, stop=True),
                reads=[("qt", e_), ("m8",)], writes=[("ps", 7)], tag=f"{tg}nm{e_}")
            p.op(AC, lambda e, e_=e_: e.activation(
                qt3[:, e_], PS[7][:, :], AF.Copy),
                reads=[("ps", 7)], writes=[("qt", e_)], tag=f"{tg}ne{e_}")
        p.op(AC, lambda e: e.activation(rz[:, :], PS[6][0:E, :], AF.Ln,
                                        bias=epsb[0:E, 1:2], scale=1.0),
             reads=[("ps", 6), ("epsD",)], writes=[("rz",)], tag=f"{tg}l1")
        p.op(AC, lambda e: e.activation(recs[:, :], rz[:, :], AF.Exp, scale=-1.0),
             reads=[("rz",)], writes=[("rec",)], tag=f"{tg}rc")
        p.op(DV, lambda e: e.tensor_copy(zb[:, :], recs[:, :]),
             reads=[("rec",)], writes=[("zb",)], tag=f"{tg}rh")
        p.op(DV, lambda e: e.tensor_copy(d1[:, :], zb[:, :]),
             reads=[("zb",)], writes=[("d1",)], tag=f"{tg}rhf")
        p.op(DV, lambda e: e.tensor_sub(rz[:, :], recs[:, :], d1[:, :]),
             reads=[("rec",), ("d1",)], writes=[("rz",)], tag=f"{tg}rl0")
        p.op(DV, lambda e: e.tensor_copy(supp[:, :], rz[:, :]),
             reads=[("rz",)], writes=[("supp",)], tag=f"{tg}rl")
        for e_ in range(NH):
            b = 2 + (e_ % 2)
            p.op(PE, lambda e, e_=e_, b=b: e.matmul(
                PS[b][:, :], sel8[:, 128 * e_:128 * (e_ + 1)], zb[:, :],
                start=True, stop=False),
                reads=[("zb",), ("sel8",)], writes=[("ps", b)], tag=f"{tg}dx{e_}")
            p.op(PE, lambda e, e_=e_, b=b: e.matmul(
                PS[b][:, :], sel8[:, 128 * e_:128 * (e_ + 1)], supp[:, :],
                start=False, stop=True),
                reads=[("supp",)], writes=[("ps", b)], tag=f"{tg}dx2{e_}")
            p.op(AC, lambda e, e_=e_, b=b: e.activation(
                t1r3[:, e_ % 2], PS[b][:, :], AF.Copy),
                reads=[("ps", b)], writes=[("t1", e_ % 2)], tag=f"{tg}dc{e_}")
            p.op(DV, lambda e, e_=e_: e.tensor_tensor(
                qt3[:, e_], qt3[:, e_], t1r3[:, e_ % 2], ALU.mult),
                reads=[("qt", e_), ("t1", e_ % 2)], writes=[("qt", e_)],
                tag=f"{tg}rt{e_}")
        for j in range(KC):
            par = wof_block(i, j)
            b = j % 2
            for e_ in range(NH):
                p.op(PE, lambda e, e_=e_, par=par, b=b: e.matmul(
                    PS[b][:, :], wfs3[par][:, e_], qt3[:, e_],
                    start=(e_ == 0), stop=(e_ == NH - 1)),
                    reads=[("qt", e_), ("wf", par)], writes=[("ps", b)],
                    tag=f"{tg}o{j}.{e_}")
            if hook is not None:
                hook(j, "pre")
            p.op(DV, lambda e, j=j, b=b: e.tensor_tensor(
                xs3[:, j], xs3[:, j], PS[b][:, :], ALU.add),
                reads=[("xs", j), ("ps", b)], writes=[("xs", j)],
                tag=f"{tg}x{j}")
            if hook is not None:
                hook(j, "post")

    xview = d_xT[:].rearrange("(k p) (i t) -> i p k t", p=128, t=T)
    yview = d_y[:].rearrange("(k p) (i t) -> i p k t", p=128, t=T)
    NSUB = 8
    SUBK = KC // NSUB

    def load_sub(i, s4):
        ks = range(s4 * SUBK, (s4 + 1) * SUBK)
        p.op(AC, lambda e, i=i, s4=s4: e.dma_start(
            out=xs3[:, s4 * SUBK:(s4 + 1) * SUBK],
            in_=xview[i][:, s4 * SUBK:(s4 + 1) * SUBK]),
            writes=[("xs", k) for k in ks], dma=f"sx{s4}", tag=f"xload{i}.{s4}")

    def store_sub(i, s4):
        ks = range(s4 * SUBK, (s4 + 1) * SUBK)
        p.op(AC, lambda e, i=i, s4=s4: e.dma_start(
            out=yview[i][:, s4 * SUBK:(s4 + 1) * SUBK],
            in_=xs3[:, s4 * SUBK:(s4 + 1) * SUBK]),
            reads=[("xs", k) for k in ks], writes=[("ystore", i, s4)],
            dma=f"yst{s4}", tag=f"ystore{i}.{s4}")

    DELAY = 1

    BATCH_STATS = False

    def mk_stats_hook(tg, extra_post=None):
        def hook(j, where):
            if where == "pre":
                if not BATCH_STATS and j >= DELAY:
                    ln_stats(tg, j - DELAY)
            else:
                if extra_post is not None:
                    extra_post(j)
                if j == KC - 1:
                    ks = range(KC) if BATCH_STATS else range(KC - DELAY, KC)
                    for k in ks:
                        ln_stats(tg, k)
        return hook

    def mk_ffn1_hook(i):
        base = mk_stats_hook(f"t{i}ln2")
        if i != 0:
            return base

        def hook(j, where):
            base(j, where)
            if where == "post" and 10 <= j < 10 + 2 * BPC and (j - 10) % 2 == 0:
                fold_block_local((j - 10) // 2)
        return hook

    def mk_attn_hook(i):
        return mk_stats_hook(f"t{i}ln3")

    def mk_ffn2_hook(i):
        LAG = 7

        def hook(j, where):
            if where == "pre":
                if i + 1 < NT:
                    if j >= 6 and (j - 6) % SUBK == 0 and (j - 6) // SUBK < NSUB - 1:
                        load_sub(i + 1, (j - 6) // SUBK)
                    if j >= LAG and (j - LAG) % SUBK == 0:
                        s4 = (j - LAG) // SUBK
                        for k in range(s4 * SUBK, (s4 + 1) * SUBK):
                            ln_stats(f"t{i + 1}ln1", k)
                return
            if j % SUBK == SUBK - 1:
                store_sub(i, j // SUBK)
                if j == KC - 1 and i + 1 < NT:
                    for k in range(KC - SUBK, KC):
                        p.op(AC, lambda e, i2=i + 1, k=k: e.dma_start(
                            out=xs3[:, k:k + 1], in_=xview[i2][:, k:k + 1]),
                            writes=[("xs", k)], dma=f"sx{4 + k % SUBK}",
                            tag=f"xloadc{i + 1}.{k}")
                        ln_stats(f"t{i + 1}ln1", k)
        return hook

    for s4 in range(NSUB):
        load_sub(0, s4)
    for k in range(KC):
        ln_stats("t0ln1", k)
    precast_phase()
    for i in range(NT):
        ln_finish(f"t{i}ln1")
        ffn_phase(i, 1, hook=mk_ffn1_hook(i))
        ln_finish(f"t{i}ln2")
        attn_phase(i, hook=mk_attn_hook(i))
        ln_finish(f"t{i}ln3")
        ffn_phase(i, 2, hook=mk_ffn2_hook(i))

    p.op(SY, lambda e: e.nop(),
         reads=[("ystore", i, s4) for i in range(NT) for s4 in range(NSUB)], tag="fin")

    p.resolve()

    sems = {}
    for e in ENGINES:
        sems[e] = ctx.enter_context(nc.semaphore(f"sem_{e}"))
    for s in p.streams:
        sems[s] = ctx.enter_context(nc.semaphore(f"sem_{s}"))

    with ctx:
        block = ctx.enter_context(nc.Block())
        body = p.emit(nc, sems)
        block.tensor(lambda e: body(e, PE))
        block.vector(lambda e: body(e, DV))
        block.scalar(lambda e: body(e, AC))
        block.sync(lambda e: body(e, SY))
        block.gpsimd(lambda e: body(e, G))

    return nc



def _prep_inputs(cfg, Wq, bq, Wo, M, z, Wb, bb, Wr, br, lA, lB):
    KC, NH = cfg["KC"], cfg["NH"]
    HID = KC * 128
    E = 8
    NC = cfg.get("NC", 8)
    BPC = KC // NC
    QPC = max(NH // NC, 1)
    c = np.ascontiguousarray
    sel = np.concatenate([np.arange(h * 128, (h + 1) * 128) for h in range(0, 4 * NH, 4)])
    d = {}
    wbT = Wb.T.astype(np.float32)
    wqT = Wq[sel].T.astype(np.float32)
    woT = Wo.T.astype(np.float32)
    def _strided(w, cx, nblk):
        return c(np.concatenate(
            [w[:, 128 * (cx + jj * NC):128 * (cx + jj * NC) + 128]
             for jj in range(nblk)], axis=1))
    d["_per_core"] = [
        {"WbS": _strided(wbT, cx, BPC),
         "WqS": c(wqT[:, (cx % NH) * QPC * 128:((cx % NH) * QPC + QPC) * 128]),
         "WoS": _strided(woT, cx, BPC)}
        for cx in range(NC)
    ]
    d["WrT"] = c(Wr.T.astype(np.float32))
    d["lA"] = c(lA.transpose(1, 0, 2).reshape(HID, 128).astype(np.float32))
    d["lB"] = c(lB.reshape(128, HID).astype(np.float32))
    d["Mt"] = c(M.astype(np.float32))
    zdv = np.zeros((NH, 128, E), np.float32)
    for e in range(NH):
        zdv[e, :, e % E] = z[e]
    d["Zd"] = zdv
    d["bqv"] = c(bq[sel].reshape(NH, 128).T.astype(np.float32))
    d["bbv"] = c(bb.reshape(KC, 128).T.astype(np.float32))
    d["brv"] = c(br.reshape(E, 1).astype(np.float32))
    d["c_ones128"] = np.full((128, 1), 1.0 / HID, np.float32)
    d["c_ones_r"] = np.ones((1, 128), np.float32)
    r1v = np.zeros((8, 64), np.float32)
    r2v = np.zeros((8, 64), np.float32)
    for ii in range(8):
        for jj in range(8):
            r1v[jj, 8 * ii + jj] = 1.0
            r2v[ii, 8 * ii + jj] = 1.0
    d["c_r1"] = r1v
    d["c_r2"] = r2v
    g1v = np.zeros((64, 8), np.float32)
    for ii in range(8):
        g1v[8 * ii:8 * ii + 8, ii] = 1.0
    d["c_g1"] = g1v
    e8 = np.zeros((8, 128), np.float32)
    for e in range(8):
        e8[e, 16 * e:16 * (e + 1)] = 1.0
    d["c_exp8"] = e8
    s8 = np.zeros((8, NH * 128), np.float32)
    for e in range(NH):
        s8[e % E, 128 * e:128 * (e + 1)] = 1.0
    d["c_sel8"] = s8
    return d


FULL_CFG = {"KC": 32, "NH": 8, "T": 512, "NT": 4, "NC": 8}


def kernel(L0_attn_hidden_states, L0_attn_Wq, L0_attn_bq, L0_attn_Wk, L0_attn_bk,
           L0_attn_Wv, L0_attn_bv, L0_attn_Wo, L0_attn_M, L0_attn_z,
           L0_ffn_W_base, L0_ffn_b_base, L0_ffn_W_router, L0_ffn_b_router,
           L0_ffn_lora_A, L0_ffn_lora_B):
    _dead = (L0_attn_Wk, L0_attn_bk, L0_attn_Wv, L0_attn_bv)
    cfg = FULL_CFG
    x = np.asarray(L0_attn_hidden_states, np.float32).reshape(16384, 4096)
    wmap = _prep_inputs(cfg, np.asarray(L0_attn_Wq, np.float32),
                        np.asarray(L0_attn_bq, np.float32),
                        np.asarray(L0_attn_Wo, np.float32),
                        np.asarray(L0_attn_M, np.float32),
                        np.asarray(L0_attn_z, np.float32),
                        np.asarray(L0_ffn_W_base, np.float32),
                        np.asarray(L0_ffn_b_base, np.float32),
                        np.asarray(L0_ffn_W_router, np.float32),
                        np.asarray(L0_ffn_b_router, np.float32),
                        np.asarray(L0_ffn_lora_A, np.float32),
                        np.asarray(L0_ffn_lora_B, np.float32))
    nc = build_program(cfg)
    per_core = wmap.pop("_per_core")
    in_maps = []
    rows = 16384 // NCORES
    for cidx in range(NCORES):
        m = dict(wmap)
        m.update(per_core[cidx])
        m["xT"] = np.ascontiguousarray(x[cidx * rows:(cidx + 1) * rows].T)
        in_maps.append(m)
    res = run_bass_kernel_spmd(nc, in_maps, core_ids=list(range(NCORES)))
    parts = [np.asarray(r["yT"]).T for r in res.results]
    return np.concatenate(parts, axis=0).reshape(2, 8192, 4096).astype(np.float32)


# version 26
# speedup vs baseline: 1.0066x; 1.0066x over previous
import numpy as np
import ml_dtypes
from contextlib import ExitStack

import concourse.bass as bass
import concourse.mybir as mybir
from concourse.bass_utils import run_bass_kernel_spmd

F32 = mybir.dt.float32
BF16 = mybir.dt.bfloat16
AF = mybir.ActivationFunctionType
ALU = mybir.AluOpType

NCORES = 8

PE, DV, AC, SY, G = "pe", "dv", "ac", "sy", "g"
ENGINES = (PE, DV, AC, SY, G)


class Prog:

    def __init__(self):
        self.ops = []
        self.streams = {}
        self.last_write = {}
        self.readers = {}
        self.eng_count = {e: 0 for e in ENGINES}

    def stream(self, name, bulk=False, cc=False):
        if name not in self.streams:
            self.streams[name] = {"bulk": bulk, "cc": cc, "count": 0}

    def op(self, eng, fn, reads=(), writes=(), dma=None, cc=False, tag=""):
        o = {"eng": eng, "fn": fn, "reads": tuple(reads), "writes": tuple(writes),
             "dma": dma, "cc": cc, "tag": tag}
        if dma is not None:
            self.stream(dma, cc=cc)
            self.streams[dma]["count"] += 1
            o["scnt"] = self.streams[dma]["count"]
            o["dim"] = ("s", dma)
        else:
            self.eng_count[eng] += 1
            o["ecnt"] = self.eng_count[eng]
            o["dim"] = ("e", eng)
        self.ops.append(o)
        tok = (o["dim"], o.get("scnt") or o.get("ecnt"))
        for r in o["reads"]:
            self.readers.setdefault(r, []).append(tok)
        for w in o["writes"]:
            self.last_write[w] = tok
            self.readers[w] = []
        o["tok"] = tok
        return o

    def resolve(self):
        dims = [("e", e) for e in ENGINES] + [("s", s) for s in self.streams]
        dix = {d: i for i, d in enumerate(dims)}
        nd = len(dims)
        vc_of = {}
        cov = {e: [0] * nd for e in ENGINES}
        last_write = {}
        readers = {}
        self.need_inc = set()

        for o in self.ops:
            E = o["eng"]
            deps = {}

            def add_dep(tok):
                (dim, cnt) = tok
                i = dix[dim]
                if deps.get(i, 0) < cnt:
                    deps[i] = cnt

            for r in o["reads"]:
                t = last_write.get(r)
                if t:
                    add_dep(t)
            for w in o["writes"]:
                t = last_write.get(w)
                if t:
                    add_dep(t)
                for t2 in readers.get(w, ()):
                    add_dep(t2)

            waits = []
            mycov = cov[E]
            for i, cnt in sorted(deps.items()):
                dim = dims[i]
                if dim == ("e", PE) and E == PE:
                    continue
                if mycov[i] >= cnt:
                    continue
                waits.append((dim, cnt))
                if dim[0] == "e":
                    self.need_inc.add((dim[1], cnt))
                elif self.streams[dim[1]]["bulk"]:
                    mycov[i] = self.streams[dim[1]]["count"]
                pvc = vc_of[(dim, cnt)]
                for k in range(nd):
                    if pvc[k] > mycov[k]:
                        mycov[k] = pvc[k]
            o["waits"] = waits

            if o["dma"] is not None and not self.streams[o["dma"]]["bulk"] \
                    and not self.streams[o["dma"]].get("cc"):
                si = dix[("s", o["dma"])]
                prev = o["scnt"] - 1
                if prev > 0 and mycov[si] < prev:
                    o["waits"].append((("s", o["dma"]), prev))
                    mycov[si] = prev

            myvc = list(mycov)
            di = dix[o["dim"]]
            myvc[di] = o.get("scnt") or o.get("ecnt")
            vc_of[o["tok"]] = tuple(myvc)

            for r in o["reads"]:
                readers.setdefault(r, []).append(o["tok"])
            for w in o["writes"]:
                last_write[w] = o["tok"]
                readers[w] = []

        self.inc_val = {}
        counts = {e: 0 for e in ENGINES}
        per_eng_idx = {e: 0 for e in ENGINES}
        for o in self.ops:
            if o["dma"] is None:
                e = o["eng"]
                per_eng_idx[e] += 1
                if (e, per_eng_idx[e]) in self.need_inc:
                    counts[e] += 1
                    self.inc_val[(e, per_eng_idx[e])] = counts[e]
        self.eng_totals = counts

    def wait_value(self, dim, cnt):
        if dim[0] == "e":
            return self.inc_val[(dim[1], cnt)]
        s = self.streams[dim[1]]
        if s.get("cc"):
            return cnt
        if s["bulk"]:
            return 16 * s["count"]
        return 16 * cnt

    def emit(self, nc, sems):
        per_eng_ops = {e: [] for e in ENGINES}
        for o in self.ops:
            per_eng_ops[o["eng"]].append(o)

        def body(eng_obj, eng_id):
            ecnt = 0
            for o in per_eng_ops[eng_id]:
                for (dim, cnt) in o["waits"]:
                    sem = sems[dim[1]]
                    eng_obj.wait_ge(sem, self.wait_value(dim, cnt))
                try:
                    ins = o["fn"](eng_obj)
                except Exception:
                    import sys
                    print(f"EMIT FAIL at op tag={o['tag']} eng={eng_id}", file=sys.stderr)
                    raise
                if o["dma"] is not None:
                    ins.then_inc(sems[o["dma"]], 1 if o["cc"] else 16)
                else:
                    ecnt += 1
                    v = self.inc_val.get((eng_id, ecnt))
                    if v is not None:
                        ins.then_inc(sems[eng_id], 1)
        return body


def build_program(cfg):
    KC = cfg["KC"]
    HID = KC * 128
    NH = cfg["NH"]
    T = cfg["T"]
    NT = cfg["NT"]
    NC = cfg.get("NC", 8)
    BPC = KC // NC
    QPC = max(NH // NC, 1)
    TOK = T * NT
    E, R = 8, 16
    QD = NH * 128
    WCH = max(KC, 4 * NH)

    nc = bass.Bass("TRN2", target_bir_lowering=False)

    d_xT = nc.dram_tensor("xT", [HID, TOK], F32, kind="ExternalInput")
    d_wr = nc.dram_tensor("WrT", [HID, E], F32, kind="ExternalInput")
    d_la = nc.dram_tensor("lA", [HID, 128], F32, kind="ExternalInput")
    d_lb = nc.dram_tensor("lB", [128, HID], F32, kind="ExternalInput")
    d_m = nc.dram_tensor("Mt", [NH, 128, 128], F32, kind="ExternalInput")
    d_zd = nc.dram_tensor("Zd", [NH, 128, E], F32, kind="ExternalInput")
    d_bq = nc.dram_tensor("bqv", [128, NH], F32, kind="ExternalInput")
    d_bb = nc.dram_tensor("bbv", [128, KC], F32, kind="ExternalInput")
    d_br = nc.dram_tensor("brv", [E, 1], F32, kind="ExternalInput")
    d_c1 = nc.dram_tensor("c_ones128", [128, 1], F32, kind="ExternalInput")
    d_cr1 = nc.dram_tensor("c_r1", [8, 64], F32, kind="ExternalInput")
    d_cr2 = nc.dram_tensor("c_r2", [8, 64], F32, kind="ExternalInput")
    d_cg = nc.dram_tensor("c_g1", [64, E], F32, kind="ExternalInput")
    d_ce = nc.dram_tensor("c_exp8", [E, 128], F32, kind="ExternalInput")
    d_cs = nc.dram_tensor("c_sel8", [E, NH * 128], F32, kind="ExternalInput")
    d_co = nc.dram_tensor("c_ones_r", [1, 128], F32, kind="ExternalInput")
    d_y = nc.dram_tensor("yT", [HID, TOK], F32, kind="ExternalOutput")
    d_wof = nc.dram_tensor("WOF", [KC, 128, NH * 128], BF16, kind="Internal",
                           addr_space="Shared")
    d_wb2 = nc.dram_tensor("WB2", [KC, 128, KC * 128], BF16, kind="Internal",
                           addr_space="Shared")
    d_wq2 = nc.dram_tensor("WQ2", [max(NH, NC), 128, KC * 128], BF16, kind="Internal",
                           addr_space="Shared")
    d_wbs = nc.dram_tensor("WbS", [HID, BPC * 128], F32, kind="ExternalInput")
    d_wqs = nc.dram_tensor("WqS", [HID, QPC * 128], F32, kind="ExternalInput")
    d_wos = nc.dram_tensor("WoS", [4 * QD, BPC * 128], F32, kind="ExternalInput")
    d_wb2s = nc.dram_tensor("WB2S", [BPC, 128, KC * 128], BF16, kind="Internal")
    d_wq2s = nc.dram_tensor("WQ2S", [QPC, 128, KC * 128], BF16, kind="Internal")
    d_wofs = nc.dram_tensor("WOFS", [BPC, 128, NH * 128], BF16, kind="Internal")

    p = Prog()
    for s in ("cb", "gb"):
        p.stream(s, bulk=True)
    for s in ([f"ws{k}g" for k in range(3)]
              + [f"ws{k}a" for k in range(3)] + [f"ws{k}b" for k in range(3)]
              + [f"ws{k}c" for k in range(3)]
              + ["wfA", "wfB", "wfw"] + [f"wf{k}" for k in range(3)]
              + [f"sx{k}" for k in range(8)] + [f"yst{k}" for k in range(8)]):
        p.stream(s)

    ctx = ExitStack()
    sb = lambda name, shape, dt: ctx.enter_context(nc.sbuf_tensor(name, shape, dt))
    pst = lambda name: ctx.enter_context(nc.psum_tensor(name, [128, T], F32))

    xs = sb("xs", [128, KC * T], F32)
    xn = sb("xn", [128, KC * T], BF16)
    xq = sb("xq", [128, 2 * T], BF16)
    accs = sb("accs", [128, T], F32)
    accq = sb("accq", [128, T], F32)
    ash = sb("ash", [128, T], BF16)
    asl = sb("asl", [128, T], BF16)
    aqh = sb("aqh", [128, T], BF16)
    aql = sb("aql", [128, T], BF16)
    tmp = sb("tmp", [128, T], BF16)
    nmx = sb("nmx", [128, T], BF16)
    tmpf32 = sb("tmpf32", [128, T], F32)
    wsb = [sb(f"ws{k}", [128, WCH * 128], BF16) for k in range(3)]
    WFW = max(NH, 8)
    wfsb = [sb(f"wfs{k}", [128, WFW * 128], BF16) for k in range(3)]
    wofsb = sb("wofsb", [128, NH * 128], BF16)
    t1r = sb("t1r", [128, 2 * T], F32)
    qt = sb("qt", [128, NH * T], BF16)
    wexp = sb("wexp", [128, T], BF16)
    hw = sb("hw", [128, T], BF16)
    la = sb("la", [128, KC * 128], BF16)
    lb = sb("lb", [128, KC * 128], BF16)
    wrs = sb("wrs", [128, KC * E], BF16)
    m8 = sb("m8", [128, NH * 128], BF16)
    zd8 = sb("zd8", [128, NH * E], BF16)
    bqs = sb("bqs", [128, NH], F32)
    bbs = sb("bbs", [128, KC], F32)
    hbb = sb("hbb", [128, KC], F32)
    brs = sb("brs", [E, 1], F32)
    epsb = sb("epsb", [128, 2], F32)
    ones128 = sb("ones128", [128, 1], BF16)
    onesr = sb("onesr", [1, 128], BF16)
    r1s = sb("r1s", [8, 64], BF16)
    r2s = sb("r2s", [8, 64], BF16)
    g1 = sb("g1", [64, E], BF16)
    exp8 = sb("exp8", [E, 128], BF16)
    sel8 = sb("sel8", [E, NH * 128], BF16)
    ones8 = sb("ones8", [E, 1], BF16)
    mu_t = sb("mu_t", [1, T], F32)
    msq_t = sb("msq_t", [1, T], F32)
    rstd_t = sb("rstd_t", [1, T], F32)
    nm_t = sb("nm_t", [1, T], F32)
    tmpf_t = sb("tmpf_t", [1, T], F32)
    t1s_t = rstd_t
    kzr = nm_t
    shi = sb("shi", [1, T], BF16)
    slo = sb("slo", [1, T], BF16)
    nhi = sb("nhi", [1, T], BF16)
    nlo = sb("nlo", [1, T], BF16)
    thi = sb("thi", [1, T], BF16)
    tlo = sb("tlo", [1, T], BF16)
    zb = sb("zb", [E, T], BF16)
    ge = sb("ge", [64, T], BF16)
    az = sb("az", [64, T], BF16)
    rz = sb("rz", [E, T], F32)
    supp = sb("supp", [E, T], BF16)
    zsup = sb("zsup", [E, T], BF16)
    d1 = sb("d1", [E, T], F32)
    wsp = sb("wsp", [E, T], BF16)
    recs = sb("recs", [E, T], F32)

    PS = [pst(f"PS{b}") for b in range(8)]

    xs3 = xs[:].rearrange("p (k t) -> p k t", t=T)
    xn3 = xn[:].rearrange("p (k t) -> p k t", t=T)
    xq3 = xq[:].rearrange("p (k t) -> p k t", t=T)
    ws3 = [w[:].rearrange("p (k c) -> p k c", c=128) for w in wsb]
    wfs3 = [w[:, 0:NH * 128].rearrange("p (e c) -> p e c", c=128) for w in wfsb]
    t1r3 = t1r[:].rearrange("p (i t) -> p i t", t=T)
    qt3 = qt[:].rearrange("p (e t) -> p e t", t=T)
    la3 = la[:].rearrange("p (k c) -> p k c", c=128)
    lb3 = lb[:].rearrange("p (k c) -> p k c", c=128)
    wr3 = wrs[:].rearrange("p (k c) -> p k c", c=E)
    m83 = m8[:].rearrange("p (e c) -> p e c", c=128)
    zd83 = zd8[:].rearrange("p (e c) -> p e c", c=E)

    def ld(out_ap, in_ap, writes, tag):
        p.op(SY, lambda e, o=out_ap, i=in_ap: e.dma_start(out=o, in_=i),
             writes=writes, dma="cb", tag=tag)

    ld(bqs[:, :], d_bq[:, :], [("bqs",)], "bq")
    ld(bbs[:, :], d_bb[:, :], [("bbs",)], "bb")
    ld(brs[:, :], d_br[:, :], [("brs",)], "br")

    def gld(out_ap, in_ap, writes, tag):
        p.op(G, lambda e, o=out_ap, i=in_ap: e.dma_start(out=o, in_=i),
             writes=writes, dma="gb", tag=tag)

    gld(ones128[:, :], d_c1[:, :], [("ones128",)], "c1")
    gld(onesr[:, :], d_co[:, :], [("onesr",)], "co")
    gld(r1s[:, :], d_cr1[:, :], [("r1s",)], "cr1")
    gld(r2s[:, :], d_cr2[:, :], [("r2s",)], "cr2")
    gld(g1[:, :], d_cg[:, :], [("g1",)], "cg")
    gld(exp8[:, :], d_ce[:, :], [("exp8",)], "ce")
    gld(sel8[:, :], d_cs[:, :], [("sel8",)], "cs")
    gld(la3, d_la[:].rearrange("(k p) c -> p k c", p=128), [("la",)], "la")
    gld(lb[:, :], d_lb[:, :], [("lb",)], "lb")
    gld(wr3, d_wr[:].rearrange("(k p) c -> p k c", p=128), [("wrs",)], "wr")
    gld(m83, d_m[:].rearrange("e p c -> p e c"), [("m8",)], "m")
    gld(zd83, d_zd[:].rearrange("e p c -> p e c"), [("zd8",)], "zd")

    p.op(DV, lambda e: e.memset(epsb[:, 0:1], 1e-5), writes=[("epsL",)])
    p.op(DV, lambda e: e.memset(epsb[:, 1:2], 1e-6), writes=[("epsD",)])
    p.op(DV, lambda e: e.memset(ones8[:, :], 1.0), writes=[("ones8",)])
    p.op(DV, lambda e: e.tensor_scalar_mul(hbb[:, :], bbs[:, :], 0.5),
         reads=[("bbs",)], writes=[("hbb",)])

    wpar = [0]

    def _next_par():
        par = wpar[0] % 3
        wpar[0] += 1
        return par

    wfpar = [0]
    wopar = [0]

    def stream_cast(src_ap, tag):
        par = _next_par()
        nch = src_ap.shape[1]
        p.op(G, lambda e, o=ws3[par][:, 0:nch], i=src_ap: e.dma_start(out=o, in_=i),
             writes=[("ws", par, 0), ("ws", par, 1)], dma=f"ws{par}g", tag=tag)
        return par

    def spill(par, nch, scratch_ap, res, sem, tag):
        p.op(G, lambda e, o=scratch_ap, i=ws3[par].rearrange("p k c -> p (k c)")[:, 0:nch * 128]:
             e.dma_start(out=o, in_=i),
             reads=[("ws", par, 0), ("ws", par, 1)], writes=[res], dma=sem, tag=tag)

    def stream_scratch(scratch_ap, res_a, res_b, nch, tag, second_q=G):
        par = _next_par()
        hh = (nch // 2) * 128
        flat = ws3[par].rearrange("p k c -> p (k c)")
        sflat = scratch_ap.rearrange("p c -> p c") if len(scratch_ap.shape) == 2 else scratch_ap
        p.op(SY, lambda e, o=flat[:, 0:hh], i=sflat[:, 0:hh]: e.dma_start(out=o, in_=i),
             reads=[res_a], writes=[("ws", par, 0)], dma=f"ws{par}a", tag=tag + "a")
        sb_name = f"ws{par}b" if second_q is G else f"ws{par}c"
        p.op(second_q, lambda e, o=flat[:, hh:2 * hh], i=sflat[:, hh:2 * hh]: e.dma_start(out=o, in_=i),
             reads=[res_b], writes=[("ws", par, 1)], dma=sb_name, tag=tag + "b")
        return par

    def stream_wb(i, call, j):
        q2 = SY if (i == 0 and call == 1) else G
        return stream_scratch(d_wb2[j], ("wb2", j, 0), ("wb2", j, 1), KC,
                              f"t{i}f{call}wb{j}", second_q=q2)

    def stream_wq(i, e_):
        return stream_scratch(d_wq2[e_], ("wq2", e_), ("wq2", e_), KC,
                              f"t{i}wq{e_}")

    wbs_blocks = d_wbs[:].rearrange("(k p) (j c) -> j p k c", p=128, c=128)
    wqs_blocks = d_wqs[:].rearrange("(k p) (j c) -> j p k c", p=128, c=128)
    wos_blocks = d_wos[:].rearrange("(k p) (j c) -> j p k c", p=128, c=128)

    def fold_block_local(jj):
        for q in range(NH // 2):
            par = wfpar[0] % 2
            wfpar[0] += 1
            sem = "wfA" if par == 0 else "wfB"
            buf = wfsb[par]
            p.op(G, lambda e, o=buf[:, 0:8 * 128], i=wos_blocks[jj][:, 8 * q:8 * q + 8]:
                 e.dma_start(out=o, in_=i),
                 writes=[("wf", par)], dma=sem, tag=f"foldc{jj}.{q}")
            stg = buf[:].rearrange("p (el g c) -> p el g c", g=4, c=128)
            out_v = wofsb[:, 2 * q * 128:(2 * q + 2) * 128].rearrange(
                "p (el c) -> p el c", c=128)
            for g in range(3):
                in0 = out_v if g > 0 else stg[:, :, 0]
                in1 = stg[:, :, g + 1]
                p.op(DV, lambda e, a=in0, b=in1, o=out_v:
                     e.tensor_tensor(o, a, b, ALU.add),
                     reads=[("wf", par)] + ([("wofsb", q)] if g > 0 else []),
                     writes=[("wofsb", q)], tag=f"folda{jj}.{q}.{g}")
        p.op(G, lambda e, jj=jj: e.dma_start(out=d_wofs[jj], in_=wofsb[:, :]),
             reads=[("wofsb", q) for q in range(NH // 2)],
             writes=[("wofs_s", jj)], dma="wfw", tag=f"foldst{jj}")
        _emit_cc(f"cco{jj}", d_wofs[jj:jj + 1], d_wof[jj * NC:(jj + 1) * NC],
                 [("wofs_s", jj)],
                 [("wof_s", jx) for jx in range(jj * NC, (jj + 1) * NC)])

    def precast_phase():
        def wb_group(jj):
            par = stream_cast(wbs_blocks[jj], f"wbc{jj}")
            spill(par, KC, d_wb2s[jj], ("wb2s", jj), f"wbw{jj}", f"wbspill{jj}")
            _emit_cc(f"ccb{jj}", d_wb2s[jj:jj + 1], d_wb2[jj * NC:(jj + 1) * NC],
                     [("wb2s", jj)],
                     [("wb2", j, h) for j in range(jj * NC, (jj + 1) * NC)
                      for h in range(2)])
        wb_group(0)
        for jj in range(QPC):
            par = stream_cast(wqs_blocks[jj], f"wqc{jj}")
            spill(par, KC, d_wq2s[jj], ("wq2s", jj), f"wqw{jj}", f"wqspill{jj}")
        _emit_cc("ccq", d_wq2s, d_wq2, [("wq2s", jj) for jj in range(QPC)],
                 [("wq2", e_) for e_ in range(NH)])
        for jj in range(1, BPC):
            wb_group(jj)

    def _emit_cc(name, src_t, dst_t, rds, wrs):
        p.op(G, lambda e, s=src_t, d=dst_t: e.collective_compute(
            "AllGather", ALU.bypass, replica_groups=[list(range(NC))],
            ins=[s[:].rearrange("j p c -> (j p) c")],
            outs=[d[:].rearrange("j p c -> (j p) c")]),
            reads=rds, writes=wrs, dma=name, cc=True, tag=name)


    def wof_block(i, j):
        par = wopar[0] % 3
        q = SY if wopar[0] % 2 == 0 else G
        wopar[0] += 1
        p.op(q, lambda e, j=j, d=wfsb[par]: e.dma_start(
            out=d[:, 0:NH * 128], in_=d_wof[j]),
            reads=[("wof_s", j)], writes=[("wf", par)], dma=f"wf{par}", tag=f"wof{i}.{j}")
        return par

    def ln_stats(tg, k):
        rk = k % 2
        p.op(AC, lambda e, k=k, rk=rk: e.activation(xq3[:, rk], xs3[:, k], AF.Square),
             reads=[("xs", k)], writes=[("xq", rk)], tag=f"{tg}q{k}")
        if k == 0:
            p.op(DV, lambda e, k=k: e.tensor_copy(accs[:, :], xs3[:, k]),
                 reads=[("xs", k)], writes=[("accs",)], tag=f"{tg}s{k}")
            p.op(DV, lambda e, rk=rk: e.tensor_copy(accq[:, :], xq3[:, rk]),
                 reads=[("xq", rk)], writes=[("accq",)], tag=f"{tg}sq{k}")
        else:
            p.op(DV, lambda e, k=k: e.tensor_tensor(
                accs[:, :], accs[:, :], xs3[:, k], ALU.add),
                reads=[("xs", k), ("accs",)], writes=[("accs",)], tag=f"{tg}s{k}")
            p.op(DV, lambda e, rk=rk: e.tensor_tensor(
                accq[:, :], accq[:, :], xq3[:, rk], ALU.add),
                reads=[("xq", rk), ("accq",)], writes=[("accq",)], tag=f"{tg}sq{k}")

    def ln_finish(tg):
        p.op(DV, lambda e: e.tensor_copy(ash[:, :], accs[:, :]),
             reads=[("accs",)], writes=[("ash",)], tag=f"{tg}ah")
        p.op(DV, lambda e: e.tensor_tensor(asl[:, :], accs[:, :], ash[:, :],
                                           ALU.subtract),
             reads=[("accs",), ("ash",)], writes=[("asl",)], tag=f"{tg}al")
        p.op(DV, lambda e: e.tensor_copy(aqh[:, :], accq[:, :]),
             reads=[("accq",)], writes=[("aqh",)], tag=f"{tg}qh")
        p.op(DV, lambda e: e.tensor_tensor(aql[:, :], accq[:, :], aqh[:, :],
                                           ALU.subtract),
             reads=[("accq",), ("aqh",)], writes=[("aql",)], tag=f"{tg}ql")
        p.op(PE, lambda e: e.matmul(PS[2][0:1, :], ones128[:, :], ash[:, :],
                                    start=True, stop=False),
             reads=[("ash",), ("ones128",)], writes=[("ps", 2)], tag=f"{tg}s1")
        p.op(PE, lambda e: e.matmul(PS[2][0:1, :], ones128[:, :], asl[:, :],
                                    start=False, stop=True),
             reads=[("asl",)], writes=[("ps", 2)], tag=f"{tg}s2")
        p.op(PE, lambda e: e.matmul(PS[3][0:1, :], ones128[:, :], aqh[:, :],
                                    start=True, stop=False),
             reads=[("aqh",)], writes=[("ps", 3)], tag=f"{tg}s3")
        p.op(PE, lambda e: e.matmul(PS[3][0:1, :], ones128[:, :], aql[:, :],
                                    start=False, stop=True),
             reads=[("aql",)], writes=[("ps", 3)], tag=f"{tg}s4")
        p.op(DV, lambda e: e.tensor_scalar_mul(mu_t[:, :], PS[2][0:1, :], 1.0),
             reads=[("ps", 2)], writes=[("mu",)], tag=f"{tg}mu")
        p.op(DV, lambda e: e.tensor_mul(tmpf_t[:, :], mu_t[:, :], mu_t[:, :]),
             reads=[("mu",)], writes=[("tmpf",)], tag=f"{tg}m2")
        p.op(DV, lambda e: e.tensor_sub(tmpf_t[:, :], PS[3][0:1, :], tmpf_t[:, :]),
             reads=[("ps", 3), ("tmpf",)], writes=[("tmpf",)], tag=f"{tg}var")
        p.op(DV, lambda e: e.tensor_scalar_add(tmpf_t[:, :], tmpf_t[:, :], 1e-5),
             reads=[("tmpf",)], writes=[("tmpf",)], tag=f"{tg}ve")
        p.op(DV, lambda e: e.reciprocal(msq_t[:, :], tmpf_t[:, :]),
             reads=[("tmpf",)], writes=[("msq",)], tag=f"{tg}rc")
        p.op(AC, lambda e: e.activation(rstd_t[:, :], msq_t[:, :], AF.Sqrt),
             reads=[("msq",)], writes=[("rstd",)], tag=f"{tg}rs")
        p.op(DV, lambda e: e.tensor_copy(shi[:, :], rstd_t[:, :]),
             reads=[("rstd",)], writes=[("rstd_hi",)], tag=f"{tg}h1")
        p.op(DV, lambda e: e.tensor_sub(slo[:, :], rstd_t[:, :], shi[:, :]),
             reads=[("rstd",), ("rstd_hi",)], writes=[("rstd_lo",)], tag=f"{tg}h3")
        p.op(PE, lambda e: e.matmul(PS[4][:, :], onesr[:, :], shi[:, :],
                                    start=True, stop=False),
             reads=[("rstd_hi",), ("onesr",)], writes=[("ps", 4)], tag=f"{tg}e1")
        p.op(PE, lambda e: e.matmul(PS[4][:, :], onesr[:, :], slo[:, :],
                                    start=False, stop=True),
             reads=[("rstd_lo",)], writes=[("ps", 4)], tag=f"{tg}e2")
        p.op(DV, lambda e: e.scalar_tensor_tensor(nm_t[:, :], mu_t[:, :], -1.0,
                                                  rstd_t[:, :], ALU.mult, ALU.mult),
             reads=[("mu",), ("rstd",)], writes=[("nm",)], tag=f"{tg}nm")
        p.op(DV, lambda e: e.tensor_copy(nhi[:, :], nm_t[:, :]),
             reads=[("nm",)], writes=[("nm_hi",)], tag=f"{tg}h4")
        p.op(DV, lambda e: e.tensor_sub(nlo[:, :], nm_t[:, :], nhi[:, :]),
             reads=[("nm",), ("nm_hi",)], writes=[("nm_lo",)], tag=f"{tg}h6")
        p.op(PE, lambda e: e.matmul(PS[5][:, :], onesr[:, :], nhi[:, :],
                                    start=True, stop=False),
             reads=[("nm_hi",)], writes=[("ps", 5)], tag=f"{tg}e3")
        p.op(PE, lambda e: e.matmul(PS[5][:, :], onesr[:, :], nlo[:, :],
                                    start=False, stop=True),
             reads=[("nm_lo",)], writes=[("ps", 5)], tag=f"{tg}e4")
        p.op(AC, lambda e: e.activation(nmx[:, :], PS[5][:, :], AF.Copy),
             reads=[("ps", 5)], writes=[("nmx",)], tag=f"{tg}nmx")
        for k in range(KC):
            p.op(DV, lambda e, k=k: e.tensor_tensor(tmp[:, :], xs3[:, k], PS[4][:, :],
                                                    ALU.mult),
                 reads=[("xs", k), ("ps", 4)], writes=[("tmp",)], tag=f"{tg}x1.{k}")
            p.op(DV, lambda e, k=k: e.tensor_tensor(xn3[:, k], tmp[:, :], nmx[:, :],
                                                    ALU.add),
                 reads=[("tmp",), ("nmx",)], writes=[("xn", k)], tag=f"{tg}x2.{k}")

    def ffn_phase(i, call, hook=None):
        tg = f"t{i}f{call}"
        for k in range(KC):
            p.op(PE, lambda e, k=k: e.matmul(PS[6][0:E, :], wr3[:, k], xn3[:, k],
                                             start=(k == 0), stop=(k == KC - 1)),
                 reads=[("xn", k), ("wrs",)], writes=[("ps", 6)], tag=f"{tg}r{k}")
            p.op(PE, lambda e, k=k: e.matmul(PS[7][:, :], la3[:, k], xn3[:, k],
                                             start=(k == 0), stop=(k == KC - 1)),
                 reads=[("xn", k), ("la",)], writes=[("ps", 7)], tag=f"{tg}h{k}")
        p.op(AC, lambda e: e.activation(zb[:, :], PS[6][0:E, :], AF.Identity,
                                        bias=brs[:, :], scale=1.0),
             reads=[("ps", 6), ("brs",)], writes=[("zb",)], tag=f"{tg}z")
        p.op(PE, lambda e: e.matmul(PS[2][0:64, :], r1s[:, :], zb[:, :],
                                    start=True, stop=True),
             reads=[("zb",), ("r1s",)], writes=[("ps", 2)], tag=f"{tg}A")
        p.op(PE, lambda e: e.matmul(PS[3][0:64, :], r2s[:, :], zb[:, :],
                                    start=True, stop=True),
             reads=[("zb",), ("r2s",)], writes=[("ps", 3)], tag=f"{tg}B")
        p.op(AC, lambda e: e.activation(az[:, :], PS[3][0:64, :], AF.Copy),
             reads=[("ps", 3)], writes=[("az",)], tag=f"{tg}Bs")
        p.op(DV, lambda e: e.tensor_tensor(ge[:, :], PS[2][0:64, :], az[:, :], ALU.is_ge),
             reads=[("ps", 2), ("az",)], writes=[("ge",)], tag=f"{tg}ge")
        p.op(DV, lambda e: e.tensor_tensor(az[:, :], PS[2][0:64, :], ge[:, :], ALU.mult),
             reads=[("ps", 2), ("ge",)], writes=[("az",)], tag=f"{tg}az")

        def base_chain(j, b=None):
            par = stream_wb(i, call, j)
            b = j % 2 if b is None else b
            for k in range(KC):
                p.op(PE, lambda e, k=k, par=par, b=b: e.matmul(
                    PS[b][:, :], ws3[par][:, k], xn3[:, k],
                    start=(k == 0), stop=False),
                    reads=[("xn", k), ("ws", par, k // (KC // 2))],
                    writes=[("ps", b)], tag=f"{tg}m{j}.{k}")

        def close_chain(j, b=None):
            b = j % 2 if b is None else b
            p.op(PE, lambda e, j=j, b=b: e.matmul(
                PS[b][:, :], lb3[:, j], hw[:, :], start=False, stop=True),
                reads=[("hw",), ("lb",)], writes=[("ps", b)], tag=f"{tg}lo{j}")
            if hook is not None:
                hook(j, "pre")
            s = j % 2
            p.op(AC, lambda e, j=j, b=b, s=s: e.activation(
                t1r3[:, s], PS[b][:, :], AF.Identity, bias=hbb[:, j:j + 1], scale=0.5),
                reads=[("ps", b), ("hbb",)], writes=[("t1", s)], tag=f"{tg}ev{j}")
            p.op(DV, lambda e, j=j, s=s: e.tensor_tensor(
                xs3[:, j], xs3[:, j], t1r3[:, s], ALU.add),
                reads=[("xs", j), ("t1", s)], writes=[("xs", j)],
                tag=f"{tg}x{j}")
            if hook is not None:
                hook(j, "post")

        base_chain(0)
        p.op(PE, lambda e: e.matmul(PS[4][0:E, :], g1[:, :], ge[:, :],
                                    start=True, stop=True),
             reads=[("ge",), ("g1",)], writes=[("ps", 4)], tag=f"{tg}rr")
        p.op(PE, lambda e: e.matmul(PS[4][32:32 + E, :], g1[:, :], az[:, :],
                                    start=True, stop=True),
             reads=[("az",), ("g1",)], writes=[("ps", 4)], tag=f"{tg}cc")
        p.op(DV, lambda e: e.tensor_tensor(rz[:, :], PS[4][0:E, :], zb[:, :], ALU.mult),
             reads=[("ps", 4), ("zb",)], writes=[("rz",)], tag=f"{tg}rz")
        p.op(DV, lambda e: e.scalar_tensor_tensor(supp[:, :], rz[:, :], 1.0,
                                                  PS[4][32:32 + E, :], ALU.add, ALU.is_gt),
             reads=[("rz",), ("ps", 4)], writes=[("supp",)], tag=f"{tg}sp")
        p.op(DV, lambda e: e.tensor_tensor(zsup[:, :], zb[:, :], supp[:, :], ALU.mult),
             reads=[("zb",), ("supp",)], writes=[("zsup",)], tag=f"{tg}zs")
        base_chain(1)
        p.op(PE, lambda e: e.matmul(PS[5][0:1, :], ones8[:, :], supp[:, :],
                                    start=True, stop=True),
             reads=[("supp",), ("ones8",)], writes=[("ps", 5)], tag=f"{tg}kz")
        p.op(PE, lambda e: e.matmul(PS[5][32:33, :], ones8[:, :], zsup[:, :],
                                    start=True, stop=True),
             reads=[("zsup",), ("ones8",)], writes=[("ps", 5)], tag=f"{tg}S")
        p.op(DV, lambda e: e.tensor_scalar_add(t1s_t[:, :], PS[5][32:33, :], -1.0),
             reads=[("ps", 5)], writes=[("rstd",)], tag=f"{tg}S1")
        p.op(DV, lambda e: e.reciprocal(kzr[:, :], PS[5][0:1, :]),
             reads=[("ps", 5)], writes=[("nm",)], tag=f"{tg}kr")
        p.op(DV, lambda e: e.tensor_mul(t1s_t[:, :], t1s_t[:, :], kzr[:, :]),
             reads=[("rstd",), ("nm",)], writes=[("rstd",)], tag=f"{tg}tau")
        p.op(DV, lambda e: e.tensor_copy(thi[:, :], t1s_t[:, :]),
             reads=[("rstd",)], writes=[("tau_hi",)], tag=f"{tg}th")
        p.op(DV, lambda e: e.tensor_sub(tlo[:, :], t1s_t[:, :], thi[:, :]),
             reads=[("rstd",), ("tau_hi",)], writes=[("tau_lo",)], tag=f"{tg}tl")
        base_chain(2, b=3)
        p.op(PE, lambda e: e.matmul(PS[4][64:64 + E, :], onesr[0:1, 0:E], thi[:, :],
                                    start=True, stop=False),
             reads=[("tau_hi",), ("onesr",)], writes=[("ps", 4)], tag=f"{tg}te1")
        p.op(PE, lambda e: e.matmul(PS[4][64:64 + E, :], onesr[0:1, 0:E], tlo[:, :],
                                    start=False, stop=True),
             reads=[("tau_lo",)], writes=[("ps", 4)], tag=f"{tg}te2")
        p.op(DV, lambda e: e.tensor_tensor(d1[:, :], zb[:, :], PS[4][64:64 + E, :],
                                           ALU.subtract),
             reads=[("zb",), ("ps", 4)], writes=[("d1",)], tag=f"{tg}d1")
        p.op(DV, lambda e: e.tensor_scalar_max(wsp[:, :], d1[:, :], 0.0),
             reads=[("d1",)], writes=[("wsp",)], tag=f"{tg}w")
        p.op(PE, lambda e: e.matmul(PS[2][:, :], exp8[:, :], wsp[:, :],
                                    start=True, stop=True),
             reads=[("wsp",), ("exp8",), ("az",), ("ge",)], writes=[("ps", 2)],
             tag=f"{tg}we")
        p.op(AC, lambda e: e.activation(wexp[:, :], PS[2][:, :], AF.Copy),
             reads=[("ps", 2)], writes=[("wexp",)], tag=f"{tg}wc")
        p.op(DV, lambda e: e.tensor_tensor(hw[:, :], PS[7][:, :], wexp[:, :], ALU.mult),
             reads=[("ps", 7), ("wexp",)], writes=[("hw",)], tag=f"{tg}hw")
        close_chain(0)
        close_chain(1)
        close_chain(2, b=3)
        for j in range(3, KC):
            base_chain(j)
            close_chain(j)

    def attn_phase(i, hook=None):
        tg = f"t{i}a"
        for e_ in range(NH):
            par = stream_wq(i, e_)
            b = e_ % 2
            for k in range(KC):
                p.op(PE, lambda e, k=k, par=par, b=b: e.matmul(
                    PS[b][:, :], ws3[par][:, k], xn3[:, k],
                    start=(k == 0), stop=(k == KC - 1)),
                    reads=[("xn", k), ("ws", par, k // (KC // 2))],
                    writes=[("ps", b)], tag=f"{tg}q{e_}.{k}")
            p.op(DV, lambda e, e_=e_, b=b: e.tensor_scalar(
                tmpf32[:, :], PS[b][:, :], bqs[:, e_:e_ + 1], 0.0, ALU.add, ALU.min),
                reads=[("ps", b), ("bqs",)], writes=[("tmpf32",)], tag=f"{tg}em{e_}")
            p.op(AC, lambda e: e.activation(t1r3[:, 1], tmpf32[:, :], AF.Exp),
                 reads=[("tmpf32",)], writes=[("t1", 1)], tag=f"{tg}ee{e_}")
            p.op(DV, lambda e, e_=e_, b=b: e.tensor_scalar(
                t1r3[:, 0], PS[b][:, :], bqs[:, e_:e_ + 1], 0.0, ALU.add, ALU.max),
                reads=[("ps", b), ("bqs",)], writes=[("t1", 0)], tag=f"{tg}er{e_}")
            p.op(DV, lambda e, e_=e_: e.tensor_tensor(
                qt3[:, e_], t1r3[:, 1], t1r3[:, 0], ALU.add),
                reads=[("t1", 1), ("t1", 0)], writes=[("qt", e_)], tag=f"{tg}qt{e_}")
            p.op(PE, lambda e, e_=e_: e.matmul(
                PS[6][0:E, :], zd83[:, e_], qt3[:, e_],
                start=(e_ == 0), stop=(e_ == NH - 1)),
                reads=[("qt", e_), ("zd8",)], writes=[("ps", 6)], tag=f"{tg}dn{e_}")
            p.op(PE, lambda e, e_=e_: e.matmul(
                PS[7][:, :], m83[:, e_], qt3[:, e_], start=True, stop=True),
                reads=[("qt", e_), ("m8",)], writes=[("ps", 7)], tag=f"{tg}nm{e_}")
            p.op(AC, lambda e, e_=e_: e.activation(
                qt3[:, e_], PS[7][:, :], AF.Copy),
                reads=[("ps", 7)], writes=[("qt", e_)], tag=f"{tg}ne{e_}")
        p.op(AC, lambda e: e.activation(rz[:, :], PS[6][0:E, :], AF.Ln,
                                        bias=epsb[0:E, 1:2], scale=1.0),
             reads=[("ps", 6), ("epsD",)], writes=[("rz",)], tag=f"{tg}l1")
        p.op(AC, lambda e: e.activation(recs[:, :], rz[:, :], AF.Exp, scale=-1.0),
             reads=[("rz",)], writes=[("rec",)], tag=f"{tg}rc")
        p.op(DV, lambda e: e.tensor_copy(zb[:, :], recs[:, :]),
             reads=[("rec",)], writes=[("zb",)], tag=f"{tg}rh")
        p.op(DV, lambda e: e.tensor_copy(d1[:, :], zb[:, :]),
             reads=[("zb",)], writes=[("d1",)], tag=f"{tg}rhf")
        p.op(DV, lambda e: e.tensor_sub(rz[:, :], recs[:, :], d1[:, :]),
             reads=[("rec",), ("d1",)], writes=[("rz",)], tag=f"{tg}rl0")
        p.op(DV, lambda e: e.tensor_copy(supp[:, :], rz[:, :]),
             reads=[("rz",)], writes=[("supp",)], tag=f"{tg}rl")
        for e_ in range(NH):
            b = 2 + (e_ % 2)
            p.op(PE, lambda e, e_=e_, b=b: e.matmul(
                PS[b][:, :], sel8[:, 128 * e_:128 * (e_ + 1)], zb[:, :],
                start=True, stop=False),
                reads=[("zb",), ("sel8",)], writes=[("ps", b)], tag=f"{tg}dx{e_}")
            p.op(PE, lambda e, e_=e_, b=b: e.matmul(
                PS[b][:, :], sel8[:, 128 * e_:128 * (e_ + 1)], supp[:, :],
                start=False, stop=True),
                reads=[("supp",)], writes=[("ps", b)], tag=f"{tg}dx2{e_}")
            p.op(AC, lambda e, e_=e_, b=b: e.activation(
                t1r3[:, e_ % 2], PS[b][:, :], AF.Copy),
                reads=[("ps", b)], writes=[("t1", e_ % 2)], tag=f"{tg}dc{e_}")
            p.op(DV, lambda e, e_=e_: e.tensor_tensor(
                qt3[:, e_], qt3[:, e_], t1r3[:, e_ % 2], ALU.mult),
                reads=[("qt", e_), ("t1", e_ % 2)], writes=[("qt", e_)],
                tag=f"{tg}rt{e_}")
        for j in range(KC):
            par = wof_block(i, j)
            b = j % 2
            for e_ in range(NH):
                p.op(PE, lambda e, e_=e_, par=par, b=b: e.matmul(
                    PS[b][:, :], wfs3[par][:, e_], qt3[:, e_],
                    start=(e_ == 0), stop=(e_ == NH - 1)),
                    reads=[("qt", e_), ("wf", par)], writes=[("ps", b)],
                    tag=f"{tg}o{j}.{e_}")
            if hook is not None:
                hook(j, "pre")
            p.op(DV, lambda e, j=j, b=b: e.tensor_tensor(
                xs3[:, j], xs3[:, j], PS[b][:, :], ALU.add),
                reads=[("xs", j), ("ps", b)], writes=[("xs", j)],
                tag=f"{tg}x{j}")
            if hook is not None:
                hook(j, "post")

    xview = d_xT[:].rearrange("(k p) (i t) -> i p k t", p=128, t=T)
    yview = d_y[:].rearrange("(k p) (i t) -> i p k t", p=128, t=T)
    NSUB = 8
    SUBK = KC // NSUB

    def load_sub(i, s4):
        ks = range(s4 * SUBK, (s4 + 1) * SUBK)
        p.op(AC, lambda e, i=i, s4=s4: e.dma_start(
            out=xs3[:, s4 * SUBK:(s4 + 1) * SUBK],
            in_=xview[i][:, s4 * SUBK:(s4 + 1) * SUBK]),
            writes=[("xs", k) for k in ks], dma=f"sx{s4}", tag=f"xload{i}.{s4}")

    def store_sub(i, s4):
        ks = range(s4 * SUBK, (s4 + 1) * SUBK)
        p.op(AC, lambda e, i=i, s4=s4: e.dma_start(
            out=yview[i][:, s4 * SUBK:(s4 + 1) * SUBK],
            in_=xs3[:, s4 * SUBK:(s4 + 1) * SUBK]),
            reads=[("xs", k) for k in ks], writes=[("ystore", i, s4)],
            dma=f"yst{s4}", tag=f"ystore{i}.{s4}")

    DELAY = 1

    BATCH_STATS = False

    def mk_stats_hook(tg, extra_post=None):
        def hook(j, where):
            if where == "pre":
                if not BATCH_STATS and j >= DELAY:
                    ln_stats(tg, j - DELAY)
            else:
                if extra_post is not None:
                    extra_post(j)
                if j == KC - 1:
                    ks = range(KC) if BATCH_STATS else range(KC - DELAY, KC)
                    for k in ks:
                        ln_stats(tg, k)
        return hook

    def mk_ffn1_hook(i):
        base = mk_stats_hook(f"t{i}ln2")
        if i != 0:
            return base

        def hook(j, where):
            base(j, where)
            if where == "post" and 10 <= j < 10 + 2 * BPC and (j - 10) % 2 == 0:
                fold_block_local((j - 10) // 2)
        return hook

    def mk_attn_hook(i):
        return mk_stats_hook(f"t{i}ln3")

    def mk_ffn2_hook(i):
        LAG = 7

        def hook(j, where):
            if where == "pre":
                if i + 1 < NT:
                    if j >= 6 and (j - 6) % SUBK == 0 and (j - 6) // SUBK < NSUB - 1:
                        load_sub(i + 1, (j - 6) // SUBK)
                    if j >= LAG and (j - LAG) % SUBK == 0:
                        s4 = (j - LAG) // SUBK
                        for k in range(s4 * SUBK, (s4 + 1) * SUBK):
                            ln_stats(f"t{i + 1}ln1", k)
                return
            if j % SUBK == SUBK - 1:
                store_sub(i, j // SUBK)
                if j == KC - 1 and i + 1 < NT:
                    for k in range(KC - SUBK, KC):
                        p.op(AC, lambda e, i2=i + 1, k=k: e.dma_start(
                            out=xs3[:, k:k + 1], in_=xview[i2][:, k:k + 1]),
                            writes=[("xs", k)], dma=f"sx{4 + k % SUBK}",
                            tag=f"xloadc{i + 1}.{k}")
                        ln_stats(f"t{i + 1}ln1", k)
        return hook

    for s4 in range(NSUB):
        load_sub(0, s4)
    for k in range(KC):
        ln_stats("t0ln1", k)
    precast_phase()
    for i in range(NT):
        ln_finish(f"t{i}ln1")
        ffn_phase(i, 1, hook=mk_ffn1_hook(i))
        ln_finish(f"t{i}ln2")
        attn_phase(i, hook=mk_attn_hook(i))
        ln_finish(f"t{i}ln3")
        ffn_phase(i, 2, hook=mk_ffn2_hook(i))

    p.op(SY, lambda e: e.nop(),
         reads=[("ystore", i, s4) for i in range(NT) for s4 in range(NSUB)], tag="fin")

    p.resolve()

    sems = {}
    for e in ENGINES:
        sems[e] = ctx.enter_context(nc.semaphore(f"sem_{e}"))
    for s in p.streams:
        sems[s] = ctx.enter_context(nc.semaphore(f"sem_{s}"))

    with ctx:
        block = ctx.enter_context(nc.Block())
        body = p.emit(nc, sems)
        block.tensor(lambda e: body(e, PE))
        block.vector(lambda e: body(e, DV))
        block.scalar(lambda e: body(e, AC))
        block.sync(lambda e: body(e, SY))
        block.gpsimd(lambda e: body(e, G))

    return nc



def _prep_inputs(cfg, Wq, bq, Wo, M, z, Wb, bb, Wr, br, lA, lB):
    KC, NH = cfg["KC"], cfg["NH"]
    HID = KC * 128
    E = 8
    NC = cfg.get("NC", 8)
    BPC = KC // NC
    QPC = max(NH // NC, 1)
    c = np.ascontiguousarray
    sel = np.concatenate([np.arange(h * 128, (h + 1) * 128) for h in range(0, 4 * NH, 4)])
    d = {}
    wbT = Wb.T.astype(np.float32)
    wqT = Wq[sel].T.astype(np.float32)
    woT = Wo.T.astype(np.float32)
    def _strided(w, cx, nblk):
        return c(np.concatenate(
            [w[:, 128 * (cx + jj * NC):128 * (cx + jj * NC) + 128]
             for jj in range(nblk)], axis=1))
    d["_per_core"] = [
        {"WbS": _strided(wbT, cx, BPC),
         "WqS": c(wqT[:, (cx % NH) * QPC * 128:((cx % NH) * QPC + QPC) * 128]),
         "WoS": _strided(woT, cx, BPC)}
        for cx in range(NC)
    ]
    d["WrT"] = c(Wr.T.astype(np.float32))
    d["lA"] = c(lA.transpose(1, 0, 2).reshape(HID, 128).astype(np.float32))
    d["lB"] = c(lB.reshape(128, HID).astype(np.float32))
    d["Mt"] = c(M.astype(np.float32))
    zdv = np.zeros((NH, 128, E), np.float32)
    for e in range(NH):
        zdv[e, :, e % E] = z[e]
    d["Zd"] = zdv
    d["bqv"] = c(bq[sel].reshape(NH, 128).T.astype(np.float32))
    d["bbv"] = c(bb.reshape(KC, 128).T.astype(np.float32))
    d["brv"] = c(br.reshape(E, 1).astype(np.float32))
    d["c_ones128"] = np.full((128, 1), 1.0 / HID, np.float32)
    d["c_ones_r"] = np.ones((1, 128), np.float32)
    r1v = np.zeros((8, 64), np.float32)
    r2v = np.zeros((8, 64), np.float32)
    for ii in range(8):
        for jj in range(8):
            r1v[jj, 8 * ii + jj] = 1.0
            r2v[ii, 8 * ii + jj] = 1.0
    d["c_r1"] = r1v
    d["c_r2"] = r2v
    g1v = np.zeros((64, 8), np.float32)
    for ii in range(8):
        g1v[8 * ii:8 * ii + 8, ii] = 1.0
    d["c_g1"] = g1v
    e8 = np.zeros((8, 128), np.float32)
    for e in range(8):
        e8[e, 16 * e:16 * (e + 1)] = 1.0
    d["c_exp8"] = e8
    s8 = np.zeros((8, NH * 128), np.float32)
    for e in range(NH):
        s8[e % E, 128 * e:128 * (e + 1)] = 1.0
    d["c_sel8"] = s8
    return d


FULL_CFG = {"KC": 32, "NH": 8, "T": 512, "NT": 4, "NC": 8}


def kernel(L0_attn_hidden_states, L0_attn_Wq, L0_attn_bq, L0_attn_Wk, L0_attn_bk,
           L0_attn_Wv, L0_attn_bv, L0_attn_Wo, L0_attn_M, L0_attn_z,
           L0_ffn_W_base, L0_ffn_b_base, L0_ffn_W_router, L0_ffn_b_router,
           L0_ffn_lora_A, L0_ffn_lora_B):
    _dead = (L0_attn_Wk, L0_attn_bk, L0_attn_Wv, L0_attn_bv)
    cfg = FULL_CFG
    x = np.asarray(L0_attn_hidden_states, np.float32).reshape(16384, 4096)
    wmap = _prep_inputs(cfg, np.asarray(L0_attn_Wq, np.float32),
                        np.asarray(L0_attn_bq, np.float32),
                        np.asarray(L0_attn_Wo, np.float32),
                        np.asarray(L0_attn_M, np.float32),
                        np.asarray(L0_attn_z, np.float32),
                        np.asarray(L0_ffn_W_base, np.float32),
                        np.asarray(L0_ffn_b_base, np.float32),
                        np.asarray(L0_ffn_W_router, np.float32),
                        np.asarray(L0_ffn_b_router, np.float32),
                        np.asarray(L0_ffn_lora_A, np.float32),
                        np.asarray(L0_ffn_lora_B, np.float32))
    nc = build_program(cfg)
    per_core = wmap.pop("_per_core")
    in_maps = []
    rows = 16384 // NCORES
    for cidx in range(NCORES):
        m = dict(wmap)
        m.update(per_core[cidx])
        m["xT"] = np.ascontiguousarray(x[cidx * rows:(cidx + 1) * rows].T)
        in_maps.append(m)
    res = run_bass_kernel_spmd(nc, in_maps, core_ids=list(range(NCORES)))
    parts = [np.asarray(r["yT"]).T for r in res.results]
    return np.concatenate(parts, axis=0).reshape(2, 8192, 4096).astype(np.float32)
